# version 13
# baseline (speedup 1.0000x reference)
import os as _os
import numpy as np
import ml_dtypes

import concourse.bass as bass
import concourse.mybir as mybir
from concourse.bass_utils import run_bass_kernel_spmd

F32 = mybir.dt.float32
BF16 = mybir.dt.bfloat16
AF = mybir.ActivationFunctionType
ALU = mybir.AluOpType
AX = mybir.AxisListType
bfnp = ml_dtypes.bfloat16

T = 2048
D = 1024
NH = 16
HD = 64
DFF = 4096
NBLK = 16
NCH = 8
NT = 16
NG = 4
FFG = 4
P = 128
KB = 1024

DBG = False
PH = 6


def _build_tables():
    half = HD // 2
    fraction = (2 * np.arange(half, dtype=np.float32)) / np.float32(HD)
    timescale = np.float32(10000.0) ** fraction
    pos = np.arange(T, dtype=np.float32)
    ang = pos[:, None] / timescale[None, :]
    sin = np.sin(ang).astype(np.float32)
    cos = np.cos(ang).astype(np.float32)
    cosdup = np.empty((P, T), dtype=np.float32)
    sinsgn = np.empty((P, T), dtype=np.float32)
    for b in range(4):
        cosdup[32 * b:32 * b + 32] = cos.T
        sinsgn[32 * b:32 * b + 32] = (-sin.T) if b % 2 == 0 else sin.T
    ind = np.zeros((P, NCH, 16), dtype=bfnp)
    for c in range(NCH):
        ind[0:64, c, 2 * c] = 1
        ind[64:128, c, 2 * c + 1] = 1
    sel = np.zeros((NBLK, NBLK * P), dtype=bfnp)
    for i in range(NBLK):
        sel[i, P * i:P * (i + 1)] = 1
    return {
        "cosdup": cosdup.astype(bfnp),
        "sinsgn": sinsgn.astype(bfnp),
        "ident": np.eye(P, dtype=bfnp),
        "causal": np.triu(np.ones((P, P), np.float32)).astype(bfnp),
        "onesrow": np.ones((1, 512), dtype=bfnp),
        "ind": ind.reshape(P, NCH * 16),
        "sel": sel,
    }


class Stream:
    def __init__(self, name, sem, inc):
        self.name = name
        self.sem = sem
        self.inc = inc
        self.count = 0
        self.thunks = []
        self.waited = {}

    def wait(self, ev):
        if ev is None:
            return
        sname, sem, val = ev
        if val <= self.waited.get(sname, -1):
            return
        self.waited[sname] = val
        self.thunks.append(lambda eng, s=sem, v=val: eng.wait_ge(s, v))

    def emit(self, fn, milestone=True, inc=None):
        if milestone:
            step = self.inc if inc is None else inc
            self.count += step

            def th(eng, fn=fn, step=step):
                fn(eng).then_inc(self.sem, step)
            self.thunks.append(th)
            return (self.name, self.sem, self.count)
        self.thunks.append(lambda eng, fn=fn: fn(eng))
        return None

    def raw(self, fn):
        self.thunks.append(fn)

    def ev(self):
        return (self.name, self.sem, self.count)


def build_nc():
    nc = bass.Bass("TRN2", target_bir_lowering=False)

    def din(name, shape, dt=F32):
        return nc.dram_tensor(name, list(shape), dt, kind="ExternalInput").ap()

    x_d = din("x", [T, D])
    wq_d = din("wq", [D, D]); wk_d = din("wk", [D, D])
    wv_d = din("wv", [D, D]); wo_d = din("wo", [D, D])
    bqp_d = din("bq_pp", [P, NCH]); bkp_d = din("bk_pp", [P, NCH])
    bv_d = din("bv_row", [1, D]); bo_d = din("bo_row", [1, D])
    rwt_d = din("rwt", [D, NBLK]); rb_d = din("rb_row", [1, NBLK])
    w1t_d = din("w1t", [D, DFF]); b1p_d = din("b1_pp", [P, 32])
    w2t_d = din("w2t", [DFF, D]); b2_d = din("b2_row", [1, D])
    cos_d = din("cosdup", [P, T], BF16); sin_d = din("sinsgn", [P, T], BF16)
    id_d = din("ident", [P, P], BF16); ca_d = din("causal", [P, P], BF16)
    ones_d = din("onesrow", [1, 512], BF16)
    ind_d = din("ind", [P, NCH * 16], BF16)
    sel_d = din("sel", [NBLK, NBLK * P], BF16)

    negc_scr_t = nc.dram_tensor("negc_scr", [NH, 1], F32, kind="Internal")
    out_d = nc.dram_tensor("out", [T, D], F32, kind="ExternalOutput").ap()
    dbg_outs = {}
    if DBG:
        for nm, shape, dt in [
            ("d_lnx", [P, NCH * T], BF16), ("d_q", [P, NCH * T], BF16),
            ("d_k", [P, NCH * T], BF16), ("d_v", [P, NT * NH * 65], BF16),
            ("d_negc", [P, NH], F32), ("d_ctx", [P, NT * D], BF16),
            ("d_smA", [P, 64], F32), ("d_smB", [P, 64], F32),
            ("d_ncb", [16, 32], BF16),
            ("d_x2", [P, NT * D], F32), ("d_mask", [NBLK, T], BF16),
            ("d_xf", [P, NCH * T], BF16),
        ]:
            dbg_outs[nm] = nc.dram_tensor(nm, shape, dt,
                                          kind="ExternalOutput").ap()

    ARENA_BYTES = 206 * KB
    arena_cm = nc.sbuf_tensor("arena", [P, ARENA_BYTES], mybir.dt.uint8)
    sems_cm = [nc.semaphore(f"sem{i}") for i in range(6)]

    with arena_cm as arena, \
            sems_cm[0] as sp, sems_cm[1] as sa, sems_cm[2] as sv, \
            sems_cm[3] as sxm, sems_cm[4] as sg, sems_cm[5] as so, \
            nc.psum_tensor("pp", [P, 4096], F32) as pp, \
            nc.Block() as block:
        a_bf = arena.bitcast(BF16)
        a_f32 = arena.bitcast(F32)
        pp_bf = pp.bitcast(BF16)

        def c_bf(off_b, n, pr=P, p0=0):
            return a_bf[p0:p0 + pr, off_b // 2: off_b // 2 + n]

        def c_f32(off_b, n, pr=P, p0=0):
            return a_f32[p0:p0 + pr, off_b // 4: off_b // 4 + n]

        O = {}
        _o = [0]

        def region(name, nbytes):
            O[name] = _o[0]
            _o[0] += (nbytes + 63) & ~63

        region("ident", 256); region("causal", 256); region("onesrow", 1024)
        region("ind", 256); region("negc", 64); region("eps", 64)
        region("clamp", 64); region("bqp", 32); region("bkp", 32)
        region("b1p", 128)
        region("bv_hi", 2 * KB); region("bv_lo", 2 * KB)
        region("bo_hi", 2 * KB); region("bo_lo", 2 * KB)
        region("b2_hi", 2 * KB); region("b2_lo", 2 * KB)
        region("rb_hi", 64); region("rb_lo", 64)
        region("smallA", 256)
        region("smallB", 256)
        region("ncbf", 64)
        region("vxf", 36 * KB)
        region("qkx2", 64 * KB)
        region("lnctx", 32 * KB)
        region("wbuf", 16 * KB)
        region("xtile", 8 * KB)
        region("cossin", 8 * KB)
        region("shufsq", 8 * KB)
        region("dmisc", 8 * KB)
        region("obuf", 8 * KB)
        assert _o[0] <= ARENA_BYTES, _o[0]

        ident = c_bf(O["ident"], P)
        causal = c_bf(O["causal"], P)
        onesrow = c_bf(O["onesrow"], 512, pr=1)
        ind = c_bf(O["ind"], NCH * 16)
        negc = c_f32(O["negc"], NH)
        eps_pp = c_f32(O["eps"], 1)
        clamp_pp = c_f32(O["clamp"], 1)
        bqp = c_f32(O["bqp"], NCH)
        bkp = c_f32(O["bkp"], NCH)
        b1p = c_f32(O["b1p"], 32)
        brow = c_f32(O["obuf"], D, pr=1)
        brow2 = c_f32(O["obuf"] + 4 * KB, D, pr=1)
        bv_hi = c_bf(O["bv_hi"], D, pr=1); bv_lo = c_bf(O["bv_lo"], D, pr=1)
        bo_hi = c_bf(O["bo_hi"], D, pr=1); bo_lo = c_bf(O["bo_lo"], D, pr=1)
        b2_hi = c_bf(O["b2_hi"], D, pr=1); b2_lo = c_bf(O["b2_lo"], D, pr=1)
        rb_hi = c_bf(O["rb_hi"], NBLK, pr=1); rb_lo = c_bf(O["rb_lo"], NBLK, pr=1)
        smallA = c_f32(O["smallA"], 64)
        smallB = c_f32(O["smallB"], 64)
        nc_bf16 = c_bf(O["ncbf"], 32, pr=16)
        nb_sb = c_f32(O["dmisc"], 2048, pr=16)
        ctx_raw = c_f32(O["dmisc"], 2 * 264).rearrange(
            "p (h c) -> p h c", h=2)

        v_tm = c_bf(O["vxf"], NT * NH * 65).rearrange(
            "p (t h c) -> p t h c", t=NT, h=NH)
        xf_fm = c_bf(O["vxf"], NCH * T).rearrange("p (c t) -> p c t", c=NCH)
        mask_fm = c_bf(O["vxf"] + 32 * KB, T, pr=NBLK)
        q_fm = c_bf(O["qkx2"], NCH * T).rearrange("p (c t) -> p c t", c=NCH)
        k_fm = c_bf(O["qkx2"] + 32 * KB, NCH * T).rearrange(
            "p (c t) -> p c t", c=NCH)
        x2 = c_f32(O["qkx2"], NT * D).rearrange("p (t d) -> p t d", t=NT)
        lnx_fm = c_bf(O["lnctx"], NCH * T).rearrange("p (c t) -> p c t", c=NCH)
        ctx_tm = c_bf(O["lnctx"], NT * D).rearrange("p (t d) -> p t d", t=NT)
        h_fm = c_bf(O["lnctx"], 32 * 512).rearrange("p (c t) -> p c t", c=32)
        wbuf = c_bf(O["wbuf"], NCH * D).rearrange("p (c d) -> p c d", c=NCH)
        xt = c_f32(O["xtile"], 2 * D).rearrange("p (b d) -> p b d", b=2)
        cos_sb = c_bf(O["cossin"], T)
        sin_sb = c_bf(O["cossin"] + 4 * KB, T)
        p_sb = c_bf(O["cossin"], 2 * 2 * 1024).rearrange(
            "p (h b c) -> p h b c", h=2, b=2)
        ln2_tm = c_bf(O["cossin"], 4 * 1024).rearrange("p (b d) -> p b d", b=4)
        rwt_sb = c_bf(O["dmisc"] + 256, NCH * NBLK).rearrange(
            "p (c r) -> p c r", c=NCH)
        shuf = c_bf(O["shufsq"], T)
        lnsq = c_bf(O["shufsq"] + 4 * KB, 2 * 1024).rearrange(
            "p (b d) -> p b d", b=2)
        sq_sb = c_bf(O["shufsq"], T)
        ctx_fm2 = c_bf(O["shufsq"], 2 * NCH * 128).rearrange(
            "p (b c t) -> p b c t", b=2, c=NCH)
        w1buf = c_bf(O["shufsq"], 2 * 1024).rearrange(
            "p (b k) -> p b k", b=2)
        w2buf = c_bf(O["shufsq"] + 4 * KB, 2 * 1024).rearrange(
            "p (b k) -> p b k", b=2)
        w1g = c_bf(O["cossin"], 2 * 8 * 512).rearrange(
            "p (b c k) -> p b c k", b=2, c=NCH)
        w2g = c_bf(O["wbuf"], 2 * 4 * 1024).rearrange(
            "p (b i k) -> p b i k", b=2, i=4)
        mask_sb = c_bf(O["dmisc"], 2 * 512).rearrange(
            "p (b k) -> p b k", b=2)
        sel_sb = c_bf(O["dmisc"] + 4 * KB, NBLK * P, pr=NBLK)
        r_sb = c_f32(O["dmisc"], 16)
        r2_sb = c_f32(O["dmisc"] + 64, 16)
        m01_sb = c_f32(O["dmisc"] + 128, 16)
        m01_bf2 = c_bf(O["dmisc"] + 192, 2 * 16).rearrange(
            "p (b r) -> p b r", b=2)
        obuf = c_f32(O["obuf"], 2 * 512).rearrange("p (b d) -> p b d", b=2)

        def bank(b, n=512, p0=0, pr=P):
            return pp[p0:p0 + pr, 512 * b: 512 * b + n]

        def bank_off(b, off, n, p0=0, pr=P):
            return pp[p0:p0 + pr, 512 * b + off: 512 * b + off + n]

        S_pe = Stream("pe", sp, 1)
        S_act = Stream("act", sa, 1)
        S_dve = Stream("dve", sv, 1)
        S_sx = Stream("sx", sxm, 16)
        S_gw = Stream("gw", sg, 16)
        S_out = S_sx
        streams = [S_pe, S_act, S_dve, S_sx, S_gw]

        def barrier():
            evs = [s.ev() for s in streams]
            for s in streams:
                for e in evs:
                    s.wait(e)

        e_x_pre = [S_sx.emit(lambda e, i=i: e.dma_start(
            out=xt[:, i, :], in_=x_d[128 * i:128 * (i + 1), :]))
            for i in range(2)]
        for dst, src in [(ident, id_d), (causal, ca_d), (onesrow, ones_d),
                         (ind, ind_d), (bqp, bqp_d), (bkp, bkp_d),
                         (b1p, b1p_d), (cos_sb, cos_d), (sin_sb, sin_d)]:
            e_tables = S_sx.emit(
                lambda e, dst=dst, src=src: e.dma_start(out=dst, in_=src))
        for s in (S_pe, S_act, S_dve, S_gw):
            s.wait(e_tables)
        S_dve.emit(lambda e: e.memset(eps_pp, 1e-5), milestone=False)
        e_eps = S_dve.emit(lambda e: e.memset(clamp_pp, 1e-30))

        wq_r = wq_d.rearrange("(c p) d -> c p d", p=P)
        wk_r = wk_d.rearrange("(c p) d -> c p d", p=P)
        wv_r = wv_d.rearrange("(c p) d -> c p d", p=P)
        wo_r = wo_d.rearrange("(c p) d -> c p d", p=P)

        def load_w(w_r):
            evs = []
            for dc in range(NCH):
                evs.append(S_gw.emit(lambda e, dc=dc, w_r=w_r: e.dma_start(
                    out=wbuf[:, dc, :], in_=w_r[dc])))
            return evs

        e_wq = load_w(wq_r)

        def mk_hilo(src_d, hi, lo, n, e_prev):
            S_gw.wait(e_prev)
            ev_l = S_gw.emit(lambda e, src_d=src_d, n=n: e.dma_start(
                out=brow[0:1, 0:n], in_=src_d))
            S_dve.wait(ev_l)
            e1 = S_dve.emit(lambda e, hi=hi, n=n: e.tensor_copy(
                hi, brow[0:1, 0:n]))
            S_dve.wait(e1)
            e2 = S_dve.emit(lambda e, hi=hi, n=n: e.tensor_copy(
                brow2[0:1, 0:n], hi))
            S_dve.wait(e2)
            return S_dve.emit(lambda e, lo=lo, n=n: e.tensor_sub(
                lo, brow[0:1, 0:n], brow2[0:1, 0:n]))

        e_bv = mk_hilo(bv_d, bv_hi, bv_lo, D, None)
        e_bo = mk_hilo(bo_d, bo_hi, bo_lo, D, e_bv)
        e_b2 = mk_hilo(b2_d, b2_hi, b2_lo, D, e_bo)
        e_rb = mk_hilo(rb_d, rb_hi, rb_lo, NBLK, e_b2)

        e_lnx_tr = [None] * NT

        def ln_stats(src, dst_tm, e_src, cb):
            S_dve.wait(e_src)
            e_sum = S_dve.emit(lambda e: e.reduce_sum(
                smallA[:, cb:cb + 1], src, axis=AX.X))
            S_act.wait(e_sum)
            e_ssq = S_act.emit(lambda e: e.activation(
                dst_tm, src, AF.Square, accum_out=smallA[:, cb + 1:cb + 2]))
            return e_sum, e_ssq

        def ln_tail(src, dst_tm, e_sum, e_ssq, cb):
            S_dve.wait(e_ssq)
            S_dve.wait(e_sum)
            ev = S_dve.emit(lambda e: e.tensor_scalar(
                smallA[:, cb + 2:cb + 3], smallA[:, cb:cb + 1], 1.0 / D,
                None, ALU.mult))
            S_dve.wait(ev)
            ev = S_dve.emit(lambda e: e.tensor_mul(
                smallA[:, cb + 3:cb + 4], smallA[:, cb + 2:cb + 3],
                smallA[:, cb + 2:cb + 3]))
            S_dve.wait(ev)
            e_var = S_dve.emit(lambda e: e.scalar_tensor_tensor(
                smallA[:, cb + 4:cb + 5], smallA[:, cb + 1:cb + 2], 1.0 / D,
                smallA[:, cb + 3:cb + 4], ALU.mult, ALU.subtract))
            S_act.wait(e_var)
            S_act.wait(e_eps)
            e_lnv = S_act.emit(lambda e: e.activation(
                smallA[:, cb + 5:cb + 6], smallA[:, cb + 4:cb + 5], AF.Ln,
                bias=eps_pp))
            S_act.wait(e_lnv)
            e_rstd = S_act.emit(lambda e: e.activation(
                smallA[:, cb + 6:cb + 7], smallA[:, cb + 5:cb + 6], AF.Exp,
                scale=-0.5))
            S_dve.wait(e_rstd)
            e_nmr = S_dve.emit(lambda e: e.tensor_scalar(
                smallA[:, cb + 7:cb + 8], smallA[:, cb + 2:cb + 3],
                smallA[:, cb + 6:cb + 7], -1.0, ALU.mult, ALU.mult))
            S_dve.wait(e_nmr)
            e_ln = S_dve.emit(lambda e: e.tensor_scalar(
                dst_tm, src, smallA[:, cb + 6:cb + 7],
                smallA[:, cb + 7:cb + 8], ALU.mult, ALU.add))
            return e_ln

        def layernorm(src_getter, dst_tm, i, e_src):
            src = src_getter()
            cb = 8 * (i % 4)
            e_sum, e_ssq = ln_stats(src, dst_tm, e_src, cb)
            return ln_tail(src, dst_tm, e_sum, e_ssq, cb)

        LNT = int(_os.environ.get("LNT", str(NT)))
        LNTR = int(_os.environ.get("LNTR", "1"))
        if PH >= 1:
         ln_rec = {}
         e_lnevac_last = [None]
         eq_map = {}
         e_qlast_mm = [None]

         def ln1_tail_of(i):
            bi = i % 2
            e_sum, e_ssq, e_x = ln_rec[i]
            e_ln_tail = ln_tail(xt[:, bi, :], lnsq[:, bi, :], e_sum, e_ssq,
                                8 * (i % 2))
            e_tr = e_ln_tail
            if LNTR:
                S_pe.wait(e_ln_tail)
                S_pe.wait(e_lnevac_last[0])
                e_trp = None
                for dc in range(NCH):
                    tcol = 1024 * (4 + dc % 4) + 128 * (dc // 4)
                    e_trp = S_pe.emit(lambda e, i=i, bi=bi, dc=dc, tcol=tcol:
                                      e.transpose(
                                          pp_bf[:, tcol:tcol + 128],
                                          lnsq[:, bi,
                                               128 * dc:128 * (dc + 1)],
                                          ident))
                S_dve.wait(e_trp)
                for dc in range(NCH):
                    tcol = 1024 * (4 + dc % 4) + 128 * (dc // 4)
                    e_tr = S_dve.emit(lambda e, i=i, dc=dc, tcol=tcol:
                                      e.tensor_copy(
                                          lnx_fm[:, dc,
                                                 128 * i:128 * (i + 1)],
                                          pp_bf[:, tcol:tcol + 128]))
                e_lnevac_last[0] = e_tr
            e_lnx_tr[i] = e_tr
            return e_ln_tail

         def emit_qg(g):
            for c in range(NCH):
                bk = (g * NCH + c) % 4
                for dc in range(NCH):
                    S_pe.wait(e_wq[dc])
                S_pe.wait(e_lnx_tr[4 * g + 3])
                S_pe.wait(eq_map.get((g * NCH + c - 4, "ev")))

                def mms(e, c=c, g=g, bk=bk):
                    for dc in range(NCH):
                        last = e.matmul(
                            bank(bk), wbuf[:, dc, 128 * c:128 * (c + 1)],
                            lnx_fm[:, dc, 512 * g:512 * (g + 1)],
                            start=(dc == 0), stop=(dc == NCH - 1))
                    return last
                e_mm = S_pe.emit(mms)
                e_qlast_mm[0] = e_mm
                S_act.wait(e_mm)
                e_ev = S_act.emit(
                    lambda e, c=c, g=g, bk=bk: e.activation(
                        q_fm[:, c, 512 * g:512 * (g + 1)],
                        bank(bk), AF.Identity, bias=bqp[:, c:c + 1]))
                eq_map[(g * NCH + c, "ev")] = e_ev
                eq_map[(c, g)] = e_ev

         e_ln_tail = None
         for i in range(LNT):
            bi = i % 2
            if i < 2:
                e_x = e_x_pre[i]
            else:
                S_sx.wait(e_ln_tail)
                e_x = S_sx.emit(lambda e, i=i, bi=bi: e.dma_start(
                    out=xt[:, bi, :], in_=x_d[128 * i:128 * (i + 1), :]))
            if i >= 2:
                S_act.wait(e_lnx_tr[i - 2])
                S_dve.wait(e_lnx_tr[i - 2])
            ln_rec[i] = (*ln_stats(xt[:, bi, :], lnsq[:, bi, :], e_x,
                                   8 * (i % 2)), e_x)
            if i >= 1:
                e_ln_tail = ln1_tail_of(i - 1)
                if PH >= 2 and (i - 1) % 4 == 3:
                    emit_qg((i - 1) // 4)
         if LNT:
            e_ln_tail = ln1_tail_of(LNT - 1)
            if PH >= 2 and LNT == NT:
                emit_qg(3)

        sq_sb2 = c_bf(O["shufsq"] + 4 * KB, T)

        wva = c_bf(O["xtile"], 4 * D).rearrange("p (c d) -> p c d", c=4)
        wvb = c_bf(O["obuf"], 4 * D).rearrange("p (c d) -> p c d", c=4)

        def wv_chunk(dc):
            return wva[:, dc, :] if dc < 4 else wvb[:, dc - 4, :]

        e_norm_last = [None]

        def emit_normchunk(src_fm, ev_src, c):
            S_dve.wait(ev_src)
            if e_norm_last[0] is not None:
                S_dve.wait(e_norm_last[0])
            if c == 0:
                S_pe.wait(e_lnx_tr[NT - 1])
            e_sq = S_dve.emit(lambda e, c=c, src_fm=src_fm: e.tensor_mul(
                sq_sb2, src_fm[:, c, :], src_fm[:, c, :]))
            S_pe.wait(e_sq)
            for tc in range(4):
                e_norm_last[0] = S_pe.emit(
                    lambda e, c=c, tc=tc: e.matmul(
                        bank(4 + tc, n=512, pr=16),
                        ind[:, 16 * c:16 * (c + 1)],
                        sq_sb2[:, 512 * tc:512 * (tc + 1)],
                        start=(c == 0), stop=(c == NCH - 1)))

        def norm_finish(dst):
            S_act.wait(e_norm_last[0])
            for tc in range(4):
                e_ev = S_act.emit(lambda e, tc=tc: e.activation(
                    nb_sb[0:16, 512 * tc:512 * (tc + 1)],
                    bank(4 + tc, pr=16), AF.Copy))
            S_dve.wait(e_ev)
            e_red = S_dve.emit(lambda e, dst=dst: e.reduce_max(
                dst[0:16, 12:13], nb_sb[0:16, :], axis=AX.X))
            return e_ev, e_red

        def emit_ropechunk(dst_fm, c, ev_src, e_prev_rope):
            S_sx.wait(ev_src)
            S_sx.wait(e_prev_rope)
            for bsrc, bdst in [(1, 0), (0, 1), (3, 2), (2, 3)]:
                e_sh = S_sx.emit(
                    lambda e, c=c, bsrc=bsrc, bdst=bdst, dst_fm=dst_fm:
                    e.dma_start(
                        out=shuf[32 * bdst:32 * (bdst + 1), :],
                        in_=dst_fm[32 * bsrc:32 * (bsrc + 1), c, :]))
            S_dve.wait(e_sh)
            S_dve.emit(lambda e: e.tensor_mul(shuf, shuf, sin_sb),
                       milestone=False)
            e_mul2 = S_dve.emit(lambda e, c=c, dst_fm=dst_fm: e.tensor_mul(
                dst_fm[:, c, :], dst_fm[:, c, :], cos_sb))
            S_dve.wait(e_mul2)
            return S_dve.emit(
                lambda e, c=c, dst_fm=dst_fm: e.tensor_add(
                    dst_fm[:, c, :], dst_fm[:, c, :], shuf))

        if PH >= 2:
          eq = [(None, eq_map[(c, g)]) for c in range(NCH) for g in range(NG)]
          S_gw.wait(e_qlast_mm[0])
          e_wk = load_w(wk_r)
          S_gw.wait(e_lnx_tr[NT - 1])
          S_gw.wait(e_rb)
          e_wv = [S_gw.emit(lambda e, dc=dc: e.dma_start(
              out=wv_chunk(dc), in_=wv_r[dc])) for dc in range(NCH)]
          e_rope_q = [None] * NCH
          e_prev_rope = [None]
          ek = []
          for g in range(NG):
              for c in range(NCH):
                  j = g * NCH + c
                  bk = j % 4
                  for dc in range(NCH):
                      S_pe.wait(e_wk[dc])
                  S_pe.wait(e_lnx_tr[4 * g + 3])
                  if len(ek) >= 4:
                      S_pe.wait(ek[-4][1])

                  def mms(e, c=c, g=g, bk=bk):
                      for dc in range(NCH):
                          last = e.matmul(
                              bank(bk),
                              wbuf[:, dc, 128 * c:128 * (c + 1)],
                              lnx_fm[:, dc, 512 * g:512 * (g + 1)],
                              start=(dc == 0), stop=(dc == NCH - 1))
                      return last
                  e_mm = S_pe.emit(mms)
                  S_act.wait(e_mm)
                  e_ev = S_act.emit(
                      lambda e, c=c, g=g, bk=bk: e.activation(
                          k_fm[:, c, 512 * g:512 * (g + 1)],
                          bank(bk), AF.Identity, bias=bkp[:, c:c + 1]))
                  ek.append((e_mm, e_ev))
                  if PH >= 3 and j % 2 == 1 and j < 16:
                      qc = j // 2
                      ev_src = eq[qc * NG + NG - 1][1]
                      emit_normchunk(q_fm, ev_src, qc)
                      e_rope_q[qc] = emit_ropechunk(
                          q_fm, qc, ev_src, e_prev_rope[0])
                      e_prev_rope[0] = e_rope_q[qc]
          if PH >= 3:
              e_qn_ev, e_qn = norm_finish(smallA)

          e_rope_k = [None] * NCH
          e_norm_last[0] = None
          ev_v = []
          for i in range(NT):
              for hf in range(2):
                  j = 2 * i + hf
                  bk = j % 4
                  for dc in range(NCH):
                      S_pe.wait(e_wv[dc])
                  S_pe.wait(e_bv)
                  S_pe.wait(e_lnx_tr[i])
                  if len(ev_v) >= 4:
                      S_pe.wait(ev_v[-4][1])

                  def mms(e, i=i, hf=hf, bk=bk):
                      for dc in range(NCH):
                          e.matmul(bank(bk),
                                   lnx_fm[:, dc, 128 * i:128 * (i + 1)],
                                   wv_chunk(dc)[:, 512 * hf:512 * (hf + 1)],
                                   start=(dc == 0), stop=False)
                      e.matmul(bank(bk), onesrow[0:1, 0:128],
                               bv_hi[0:1, 512 * hf:512 * (hf + 1)],
                               start=False, stop=False)
                      return e.matmul(bank(bk), onesrow[0:1, 0:128],
                                      bv_lo[0:1, 512 * hf:512 * (hf + 1)],
                                      start=False, stop=True)
                  e_mm = S_pe.emit(mms)
                  S_dve.wait(e_mm)
                  e_ev = S_dve.emit(lambda e, i=i, hf=hf, bk=bk: e.tensor_copy(
                      v_tm[:, i, 8 * hf:8 * (hf + 1), 0:64],
                      bank(bk).rearrange("p (h s) -> p h s", s=64)))
                  ev_v.append((e_mm, e_ev))
                  if PH >= 3 and j % 2 == 1 and j < 16:
                      kc = j // 2
                      ev_src = ek[3 * NCH + kc][1]
                      if kc == 0:
                          S_pe.wait(e_qn_ev)
                      emit_normchunk(k_fm, ev_src, kc)
                      e_rope_k[kc] = emit_ropechunk(
                          k_fm, kc, ev_src, e_prev_rope[0])
                      e_prev_rope[0] = e_rope_k[kc]
          e_vones = S_dve.emit(lambda e: e.memset(
              v_tm[:, :, :, 64:65], 1.0))
          if PH >= 5:
              S_gw.wait(ek[-1][0])
              e_wo = load_w(wo_r)

        if PH >= 3:
          S_act.wait(e_qn)
          e_kn_ev, e_kn = norm_finish(smallB)

          S_gw.wait(e_kn)
          S_gw.wait(e_qn)
          e_c2 = S_gw.emit(lambda e: e.tensor_mul(
              smallA[0:16, 13:14], smallA[0:16, 12:13], smallB[0:16, 12:13]))
          S_act.wait(e_c2)
          eps16 = c_f32(O["eps"], 1, pr=16)
          e_lnc = S_act.emit(lambda e: e.activation(
              smallA[0:16, 14:15], smallA[0:16, 13:14], AF.Ln,
              bias=eps16))
          S_act.wait(e_lnc)
          e_c8 = S_act.emit(lambda e: e.activation(
              smallA[0:16, 15:16], smallA[0:16, 14:15], AF.Exp, scale=0.5))
          S_gw.wait(e_c8)
          e_nmul = S_gw.emit(lambda e: e.tensor_scalar(
              smallB[0:16, 14:15], smallA[0:16, 15:16], -0.125, None,
              ALU.mult))
          S_sx.wait(e_nmul)
          e_nst = S_sx.emit(lambda e: e.dma_start(
              out=negc_scr_t.ap(), in_=smallB[0:16, 14:15]))
          S_sx.wait(e_nst)
          negc_bcast_src = bass.AP(negc_scr_t.ap().tensor, 0,
                                   [[0, P], [1, NH]])
          e_negc = S_sx.emit(lambda e: e.dma_start(
              out=negc, in_=negc_bcast_src))

        if DBG:
            barrier()
            taps = []
            if PH >= 1:
                taps.append(("d_lnx", c_bf(O["lnctx"], NCH * T)))
            if PH >= 2:
                taps.append(("d_v", c_bf(O["vxf"], NT * NH * 65)))
            if PH >= 3:
                taps += [("d_q", c_bf(O["qkx2"], NCH * T)),
                         ("d_k", c_bf(O["qkx2"] + 32 * KB, NCH * T)),
                         ("d_negc", negc), ("d_smA", smallA),
                         ("d_smB", smallB), ("d_ncb", nc_bf16)]
            if _os.environ.get("DBGTAP", "1") == "1":
                for nm, view in taps:
                    S_out.emit(lambda e, nm=nm, view=view: e.dma_start(
                        out=dbg_outs[nm], in_=view))

        barrier()
        if PH >= 4:
          PBBASE = [0, 2, 6]
          ABANKS = [[0, 1, 2], [3, 6, 7]]
          e_slot = {}
          e_psb = {}
          e_ctx_ev = {}
          for h01 in range(2):
              for sl in range(3):
                  e_slot[(h01, sl)] = None
              for sl in range(4):
                  e_psb[(h01, sl)] = None
          p_sbc = p_sb.rearrange("p h b c -> p (h b) c").rearrange(
              "p s (b c) -> p (s b) c", c=512)

          for c in range(NCH):
              for g in range(NG):
                  chunks = []
                  for j in range(4 * g + 4):
                      jrel = j - 4 * g
                      if jrel < 0:
                          chunks.append((j, 512, 512 * g))
                      else:
                          chunks.append((j, 512 - 128 * jrel,
                                         512 * g + 128 * jrel))
                  nj = len(chunks)

                  def emit_mm(j):
                      jj, w, qoff = chunks[j]
                      ev = {}
                      for h01 in range(2):
                          rb0 = 64 * h01
                          sb_ = ABANKS[h01][j % 3]
                          S_pe.wait(e_slot[(h01, j % 3)])
                          ev[h01] = S_pe.emit(
                              lambda e, c=c, rb0=rb0, sb_=sb_, jj=jj, w=w,
                              qoff=qoff: e.matmul(
                                  bank(sb_, n=w),
                                  k_fm[rb0:rb0 + 64, c,
                                       128 * jj:128 * (jj + 1)],
                                  q_fm[rb0:rb0 + 64, c, qoff:qoff + w],
                                  start=True, stop=True,
                                  tile_position=(rb0, 0)))
                      return ev

                  def emit_rest(j, g=g):
                      jj, w, qoff = chunks[j]
                      jrel0 = max(jj - 4 * g, 0)
                      for h01 in range(2):
                          h = 2 * c + h01
                          sb_ = ABANKS[h01][j % 3]
                          ps = 4 * h01 + j % 4
                          S_act.wait(mm_evs[j][h01])
                          S_act.wait(e_negc)
                          S_act.wait(e_psb[(h01, j % 4)])
                          e_exp = S_act.emit(
                              lambda e, h=h, sb_=sb_, ps=ps, w=w:
                              e.activation(
                                  p_sbc[:, ps, 0:w], bank(sb_, n=w),
                                  AF.Exp, bias=negc[:, h:h + 1], scale=0.125))
                          e_slot[(h01, j % 3)] = e_exp
                          e_mask = e_exp
                          if jj >= 4 * g:
                              S_dve.wait(e_exp)
                              e_mask = S_dve.emit(
                                  lambda e, ps=ps: e.tensor_mul(
                                      p_sbc[:, ps, 0:128],
                                      p_sbc[:, ps, 0:128], causal))
                          cbank = 4 + h01
                          last_pv = None
                          for t in range(jrel0, 4):
                              first = (jj == 0 and t == jrel0)
                              lastm = (jj == 4 * g + 3 and t == 3)
                              S_pe.wait(e_mask)
                              if first:
                                  S_pe.wait(e_ctx_ev.get(h01))
                              colp = (t - jrel0) * 128
                              last_pv = S_pe.emit(
                                  lambda e, h=h, ps=ps, t=t, jj=jj, colp=colp,
                                  cbank=cbank, first=first, lastm=lastm:
                                  e.matmul(
                                      bank_off(cbank, 66 * t, 65),
                                      p_sbc[:, ps, colp:colp + 128],
                                      v_tm[:, jj, h, :],
                                      start=first, stop=lastm))
                          e_psb[(h01, j % 4)] = last_pv
                          if jj == 4 * g + 3:
                              S_dve.wait(last_pv)
                              e_pk = S_dve.emit(
                                  lambda e, cbank=cbank, h01=h01:
                                  e.tensor_copy(ctx_raw[:, h01, :],
                                                bank(cbank, n=264)))
                              e_ctx_ev[h01] = e_pk
                              crv = ctx_raw[:, h01, :].rearrange(
                                  "p (t c) -> p t c", c=66)
                              S_act.wait(e_pk)
                              e_cl = S_act.emit(
                                  lambda e, crv=crv, h01=h01: e.activation(
                                      smallB[:, 16 + 8 * h01:20 + 8 * h01],
                                      crv[:, :, 64:65].rearrange(
                                          "p t c -> p (t c)"), AF.Copy))
                              S_dve.wait(e_cl)
                              S_dve.emit(
                                  lambda e, h01=h01: e.tensor_scalar(
                                      smallB[:, 16 + 8 * h01:20 + 8 * h01],
                                      smallB[:, 16 + 8 * h01:20 + 8 * h01],
                                      clamp_pp, None, ALU.max),
                                  milestone=False)
                              e_rc = S_dve.emit(
                                  lambda e, h01=h01: e.reciprocal(
                                      smallB[:, 20 + 8 * h01:24 + 8 * h01],
                                      smallB[:, 16 + 8 * h01:20 + 8 * h01]))
                              S_dve.wait(e_rc)
                              for t in range(4):
                                  S_dve.emit(
                                      lambda e, t=t, g=g, h=h, h01=h01,
                                      crv=crv: e.tensor_scalar(
                                          ctx_tm[:, 4 * g + t,
                                                 64 * h:64 * (h + 1)],
                                          crv[:, t, 0:64],
                                          smallB[:, 20 + 8 * h01 + t:
                                                 21 + 8 * h01 + t],
                                          None, ALU.mult))

                  mm_evs = {}
                  for j in range(nj + 1):
                      if j < nj:
                          mm_evs[j] = emit_mm(j)
                      if j >= 1:
                          emit_rest(j - 1)

        if DBG and PH >= 4:
            barrier()
            S_out.emit(lambda e: e.dma_start(
                out=dbg_outs["d_ctx"], in_=c_bf(O["lnctx"], NT * D)))

        if PH >= 5:
          barrier()
          rwt_r = rwt_d.rearrange("(c p) r -> c p r", p=P)
          e_rwt = []
          for dc in range(NCH):
              e_rwt.append(S_gw.emit(lambda e, dc=dc: e.dma_start(
                  out=rwt_sb[:, dc, :], in_=rwt_r[dc])))

          e_x2 = [None] * NT
          e_ln2 = [None] * NT
          e_xftr = [None] * NT
          e_ms = [None] * NT
          e_rmm = [None] * NT
          e_r2c = [None] * NT
          e_m01e = [None] * NT
          e_mtre = [None] * NT
          e_ctr_ev = {}
          e_xr_ev = {}
          e_omm_hist = {}

          def c_partA(i):
              bi = i % 2
              if i >= 2:
                  S_gw.wait(e_x2[i - 2])
              e_xr = S_gw.emit(lambda e, i=i, bi=bi: e.dma_start(
                  out=xt[:, bi, :], in_=x_d[128 * i:128 * (i + 1), :]))
              e_xr_ev[i] = e_xr
              cb = 4 + i % 2
              S_pe.wait(e_ctr_ev.get(i - 2))
              e_trp = None
              for dc in range(NCH):
                  tcol = 1024 * cb + 128 * dc
                  e_trp = S_pe.emit(lambda e, i=i, dc=dc, tcol=tcol:
                                    e.transpose(
                                        pp_bf[:, tcol:tcol + 128],
                                        ctx_tm[:, i, 128 * dc:128 * (dc + 1)],
                                        ident))
              S_dve.wait(e_trp)
              S_dve.wait(e_omm_hist.get(i - 2))
              ev_tr = None
              for dc in range(NCH):
                  tcol = 1024 * cb + 128 * dc
                  ev_tr = S_dve.emit(lambda e, i=i, dc=dc, tcol=tcol, bi=bi:
                                     e.tensor_copy(ctx_fm2[:, bi, dc, :],
                                                   pp_bf[:, tcol:tcol + 128]))
              e_ctr_ev[i] = ev_tr
              if i >= 1:
                  emit_oproj(i - 1)

          def emit_oproj(i):
              bi = i % 2
              for hf in range(2):
                  bk = hf
                  S_pe.wait(e_ctr_ev[i])
                  for dc in range(NCH):
                      S_pe.wait(e_wo[dc])
                  S_pe.wait(e_bo)
                  if i >= 1:
                      S_pe.wait(e_x2[i - 1])

                  def mms(e, hf=hf, bk=bk, bi=bi):
                      for dc in range(NCH):
                          e.matmul(bank(bk), ctx_fm2[:, bi, dc, :],
                                   wbuf[:, dc, 512 * hf:512 * (hf + 1)],
                                   start=(dc == 0), stop=False)
                      e.matmul(bank(bk), onesrow[0:1, 0:128],
                               bo_hi[0:1, 512 * hf:512 * (hf + 1)],
                               start=False, stop=False)
                      return e.matmul(bank(bk), onesrow[0:1, 0:128],
                                      bo_lo[0:1, 512 * hf:512 * (hf + 1)],
                                      start=False, stop=True)
                  e_omm = S_pe.emit(mms)
                  e_omm_hist[i] = e_omm
                  S_dve.wait(e_omm)
                  S_dve.wait(e_xr_ev[i])
                  e_x2i = S_dve.emit(lambda e, i=i, hf=hf, bk=bk,
                                     bi=bi: e.tensor_add(
                                         x2[:, i, 512 * hf:512 * (hf + 1)],
                                         bank(bk),
                                         xt[:, bi, 512 * hf:512 * (hf + 1)]))
              e_x2[i] = e_x2i
              if i >= 4:
                  S_act.wait(e_xftr[i - 4])
              e_ln2[i] = layernorm(lambda i=i: x2[:, i, :],
                                   ln2_tm[:, i % 4, :], i, e_x2[i])

          def c_partB1(i):
              xb = 6 + i % 2
              S_pe.wait(e_ln2[i])
              if i >= 2:
                  S_pe.wait(e_xftr[i - 2])
              e_trp = None
              for dc in range(NCH):
                  tcol = 1024 * xb + 128 * dc
                  e_trp = S_pe.emit(lambda e, i=i, dc=dc, tcol=tcol:
                                    e.transpose(
                                        pp_bf[:, tcol:tcol + 128],
                                        ln2_tm[:, i % 4,
                                               128 * dc:128 * (dc + 1)],
                                        ident))
              S_act.wait(e_trp)
              e_xtr = None
              for dc in range(NCH):
                  tcol = 1024 * xb + 128 * dc
                  e_xtr = S_act.emit(lambda e, i=i, dc=dc, tcol=tcol:
                                     e.activation(
                                         xf_fm[:, dc, 128 * i:128 * (i + 1)],
                                         pp_bf[:, tcol:tcol + 128], AF.Copy))
              e_xftr[i] = e_xtr

          def c_partB2a(i):
              S_pe.wait(e_xftr[i])
              for er in e_rwt:
                  S_pe.wait(er)
              S_pe.wait(e_rb)
              if i >= 1:
                  S_pe.wait(e_r2c[i - 1])

              def rmm(e, i=i):
                  for dc in range(NCH):
                      e.matmul(bank(2, n=16),
                               xf_fm[:, dc, 128 * i:128 * (i + 1)],
                               rwt_sb[:, dc, :], start=(dc == 0), stop=False)
                  e.matmul(bank(2, n=16), onesrow[0:1, 0:128], rb_hi[0:1, :],
                           start=False, stop=False)
                  return e.matmul(bank(2, n=16), onesrow[0:1, 0:128],
                                  rb_lo[0:1, :], start=False, stop=True)
              e_rmm[i] = S_pe.emit(rmm)
              S_dve.wait(e_rmm[i])
              if i >= 1:
                  S_dve.wait(e_m01e[i - 1])
              S_dve.emit(lambda e: e.tensor_copy(r_sb, bank(2, n=16)),
                         milestone=False)
              ev = S_dve.emit(lambda e: e.tensor_copy(r2_sb, bank(2, n=16)))
              e_r2c[i] = ev
              for _ in range(3):
                  S_dve.wait(ev)
                  ev = S_dve.emit(lambda e: e.reduce_max(
                      smallA[:, 32:33], r2_sb, axis=AX.X))
                  S_gw.wait(ev)
                  ev = S_gw.emit(lambda e: e.tensor_scalar(
                      m01_sb, r2_sb, smallA[:, 32:33], -1e30,
                      ALU.is_ge, ALU.mult))
                  S_gw.wait(ev)
                  ev = S_gw.emit(lambda e: e.tensor_add(
                      r2_sb, r2_sb, m01_sb))
              S_dve.wait(ev)
              ev = S_dve.emit(lambda e: e.reduce_max(
                  smallA[:, 32:33], r2_sb, axis=AX.X))
              S_gw.wait(ev)
              ev = S_gw.emit(lambda e: e.tensor_scalar(
                  m01_sb, r_sb, smallA[:, 32:33], None, ALU.is_ge))
              S_gw.wait(ev)
              if i >= 2:
                  S_gw.wait(e_mtre[i - 2])
              e_m01e[i] = S_gw.emit(lambda e, i=i: e.tensor_copy(
                  m01_bf2[:, i % 2, :], m01_sb))

          def c_partB2b(i):
              S_pe.wait(e_m01e[i])
              if i >= 1:
                  S_pe.wait(e_ms[i - 1])
              e_mtr = S_pe.emit(lambda e, i=i: e.matmul(
                  bank(3, n=128, pr=16), m01_bf2[:, i % 2, :], ident,
                  start=True, stop=True))
              e_mtre[i] = e_mtr
              S_dve.wait(e_mtr)
              e_ms[i] = S_dve.emit(lambda e, i=i: e.tensor_copy(
                  mask_fm[0:16, 128 * i:128 * (i + 1)],
                  bank(3, n=128, pr=16)))

          for it in range(NT + 7):
              if 0 <= it - 6 < NT:
                  c_partB2b(it - 6)
              if 0 <= it - 3 < NT:
                  c_partB1(it - 3)
              if it < NT:
                  c_partA(it)
              elif it == NT:
                  emit_oproj(NT - 1)
              if 0 <= it - 4 < NT:
                  c_partB2a(it - 4)
        if DBG and PH >= 5:
            barrier()
            S_out.emit(lambda e: e.dma_start(
                out=dbg_outs["d_x2"], in_=c_f32(O["qkx2"], NT * D)))
            S_out.emit(lambda e: e.dma_start(
                out=dbg_outs["d_mask"], in_=mask_fm))
            S_out.emit(lambda e: e.dma_start(
                out=dbg_outs["d_xf"], in_=c_bf(O["vxf"], NCH * T)))

        if PH >= 6:
          barrier()
          e_selload = S_sx.emit(lambda e: e.dma_start(out=sel_sb, in_=sel_d))
          S_pe.wait(e_selload)
          S_act.wait(e_selload)

          w1t_r = w1t_d.rearrange("(c p) o -> c p o", p=P)
          w2t_r = w2t_d.rearrange("(c p) d -> c p d", p=P)

          e_w1l = {}
          e_w2l = {}
          e_w1use = {}
          e_w2use = {}
          e_hev = {}
          e_mbe = [None, None]
          e_ye_bank = [None] * 8
          e_y_last_dve = None
          e_ymm_last = None
          ylast = []

          for G in range(FFG):
              t0 = 512 * G
              def load_w1(og):
                  wb = og % 2
                  S_gw.wait(e_w1use.get((G, og - 2)))
                  if G > 0 and og < 2:
                      S_gw.wait(e_w1use.get((G - 1, 6 + og)))
                  e_w1l[(G, og)] = S_gw.emit(
                      lambda e, og=og, wb=wb: e.dma_start(
                          out=w1g[:, wb, :, :],
                          in_=w1t_r[:, :, 512 * og:512 * (og + 1)].rearrange(
                              "c p o -> p c o")))

              load_w1(0)
              load_w1(1)
              for oc in range(32):
                  og = oc // 4
                  wb = og % 2
                  i_blk = oc // 2
                  mb = i_blk % 2
                  if oc % 2 == 0:
                      S_pe.wait(e_ms[4 * G + 3])
                      S_pe.wait(e_mbe[mb])
                      if G > 0 and oc == 0:
                          S_pe.wait(e_ye_bank[3])
                      e_mm = S_pe.emit(lambda e, i_blk=i_blk, t0=t0: e.matmul(
                          bank(3), sel_sb[0:16, 128 * i_blk:128 * (i_blk + 1)],
                          mask_fm[0:16, t0:t0 + 512], start=True, stop=True))
                      S_act.wait(e_mm)
                      S_act.wait(e_hev.get((G, oc - 3)))
                      if G > 0 and oc < 4:
                          S_act.wait(e_hev.get((G - 1, 28 + oc)))
                      e_mbe[mb] = S_act.emit(lambda e, mb=mb: e.activation(
                          mask_sb[:, mb, :], bank(3), AF.Copy))
                  bk = oc % 3
                  S_pe.wait(e_w1l[(G, og)])
                  S_pe.wait(e_xftr[4 * G + 3])
                  S_pe.wait(e_hev.get((G, oc - 3)))
                  if G > 0 and oc < 3:
                      S_pe.wait(e_ye_bank[oc % 3])
                      S_pe.wait(e_ymm_last)

                  def hmm(e, oc=oc, wb=wb, t0=t0, bk=bk):
                      for dc in range(NCH):
                          last = e.matmul(
                              bank(bk),
                              w1g[:, wb, dc, 128 * (oc % 4):128 * (oc % 4 + 1)],
                              xf_fm[:, dc, t0:t0 + 512],
                              start=(dc == 0), stop=(dc == NCH - 1))
                      return last
                  e_hmm = S_pe.emit(hmm)
                  e_w1use[(G, og)] = e_hmm
                  if oc % 4 == 3 and og + 2 < 8:
                      load_w1(og + 2)
                  S_act.wait(e_hmm)
                  if G > 0:
                      S_act.wait(e_ymm_last)
                  e_hr = S_act.emit(lambda e, oc=oc, bk=bk: e.activation(
                      h_fm[:, oc, :], bank(bk), AF.Relu,
                      bias=b1p[:, oc:oc + 1]))
                  S_dve.wait(e_hr)
                  S_dve.wait(e_mbe[mb])
                  e_hm = S_dve.emit(lambda e, oc=oc, mb=mb: e.tensor_mul(
                      h_fm[:, oc, :], h_fm[:, oc, :], mask_sb[:, mb, :]))
                  e_hev[(G, oc)] = e_hm

              def load_w2(ig):
                  wb = ig % 2
                  S_gw.wait(e_w2use.get((G, ig - 2)))
                  if G > 0 and ig < 2:
                      S_gw.wait(e_w2use.get((G - 1, 6 + ig)))
                  e_w2l[(G, ig)] = S_gw.emit(
                      lambda e, ig=ig, wb=wb: e.dma_start(
                          out=w2g[:, wb, :, :],
                          in_=w2t_r[4 * ig:4 * (ig + 1)].rearrange(
                              "i p d -> p i d")))

              load_w2(0)
              load_w2(1)
              for ib in range(32):
                  ig = ib // 4
                  wb = ig % 2
                  S_pe.wait(e_w2l[(G, ig)])
                  S_pe.wait(e_hev[(G, ib)])
                  if ib == 0:
                      S_pe.wait(e_hev[(G, 31)])
                      for bb in range(8):
                          S_pe.wait(e_ye_bank[bb])
                      S_pe.wait(e_mbe[0]); S_pe.wait(e_mbe[1])

                  def ymm(e, ib=ib, wb=wb):
                      last = None
                      for tt in range(4):
                          for hf in range(2):
                              last = e.matmul(
                                  bank(2 * tt + hf),
                                  h_fm[:, ib, 128 * tt:128 * (tt + 1)],
                                  w2g[:, wb, ib % 4, 512 * hf:512 * (hf + 1)],
                                  start=(ib == 0), stop=False)
                      return last
                  e_ymm = S_pe.emit(ymm)
                  e_w2use[(G, ig)] = e_ymm
                  if ib % 4 == 3 and ig + 2 < 8:
                      load_w2(ig + 2)
              e_ymm_last = e_ymm

              S_pe.wait(e_b2)

              def ybias(e):
                  last = None
                  for tt in range(4):
                      for hf in range(2):
                          bk2 = 2 * tt + hf
                          e.matmul(bank(bk2), onesrow[0:1, 0:128],
                                   b2_hi[0:1, 512 * hf:512 * (hf + 1)],
                                   start=False, stop=False)
                          last = e.matmul(bank(bk2), onesrow[0:1, 0:128],
                                          b2_lo[0:1, 512 * hf:512 * (hf + 1)],
                                          start=False, stop=True)
                  return last
              e_yb = S_pe.emit(ybias)

              n_ev = 0
              for tt in range(4):
                  i_abs = 4 * G + tt
                  for hf in range(2):
                      bk2 = 2 * tt + hf
                      ob = n_ev % 2
                      S_dve.wait(e_yb)
                      if len(ylast) >= 2:
                          S_dve.wait(ylast[-2])
                      e_ye = S_dve.emit(lambda e, i_abs=i_abs, hf=hf, bk2=bk2,
                                        ob=ob: e.tensor_add(
                                            obuf[:, ob, :], bank(bk2),
                                            x2[:, i_abs,
                                               512 * hf:512 * (hf + 1)]))
                      e_ye_bank[bk2] = e_ye
                      e_y_last_dve = e_ye
                      S_out.wait(e_ye)
                      e_st = S_out.emit(lambda e, i_abs=i_abs, hf=hf, ob=ob:
                                        e.dma_start(
                                            out=out_d[128 * i_abs:
                                                      128 * (i_abs + 1),
                                                      512 * hf:
                                                      512 * (hf + 1)],
                                            in_=obuf[:, ob, :]))
                      ylast.append(e_st)
                      n_ev += 1

        S_sx.raw(lambda eng, v=S_sx.count: eng.wait_ge(sxm, v))

        def run_stream(s):
            def body(eng):
                for th in s.thunks:
                    th(eng)
            return body

        block.tensor(run_stream(S_pe))
        block.scalar(run_stream(S_act))
        block.vector(run_stream(S_dve))

        block.sync(run_stream(S_sx))
        block.gpsimd(run_stream(S_gw))

    return nc


def kernel(**inputs):
    x = np.ascontiguousarray(np.asarray(inputs["L0_attn_x"], dtype=np.float32))

    def g(n):
        return np.ascontiguousarray(np.asarray(inputs[n], dtype=np.float32))

    wq, wk, wv, wo = (g(f"L0_attn_w{n}_w") for n in "qkvo")
    bq, bk, bv, bo = (g(f"L0_attn_w{n}_b") for n in "qkvo")
    rw, rb = g("L0_ffn_router_w"), g("L0_ffn_router_b")
    w1, b1 = g("L0_ffn_w1"), g("L0_ffn_b1")
    w2, b2 = g("L0_ffn_w2"), g("L0_ffn_b2")

    common = {
        "wq": wq, "wk": wk, "wv": wv, "wo": wo,
        "bq_pp": np.ascontiguousarray(bq.reshape(NCH, P).T),
        "bk_pp": np.ascontiguousarray(bk.reshape(NCH, P).T),
        "bv_row": bv.reshape(1, D).copy(), "bo_row": bo.reshape(1, D).copy(),
        "rwt": np.ascontiguousarray(rw.T), "rb_row": rb.reshape(1, NBLK).copy(),
        "w1t": np.ascontiguousarray(w1.T),
        "b1_pp": np.ascontiguousarray(b1.reshape(32, P).T),
        "w2t": np.ascontiguousarray(w2.T), "b2_row": b2.reshape(1, D).copy(),
    }
    common.update(_build_tables())

    nc = build_nc()
    in_maps = []
    for core in range(8):
        m = dict(common)
        m["x"] = np.ascontiguousarray(x[core])
        in_maps.append(m)
    res = run_bass_kernel_spmd(nc, in_maps, core_ids=list(range(8)))
    return np.stack([r["out"] for r in res.results], axis=0)


# version 14 (final)
# speedup vs baseline: 1.0436x; 1.0436x over previous
import os as _os
import numpy as np
import ml_dtypes

import concourse.bass as bass
import concourse.mybir as mybir
from concourse.bass_utils import run_bass_kernel_spmd

F32 = mybir.dt.float32
BF16 = mybir.dt.bfloat16
AF = mybir.ActivationFunctionType
ALU = mybir.AluOpType
AX = mybir.AxisListType
bfnp = ml_dtypes.bfloat16

T = 2048
D = 1024
NH = 16
HD = 64
DFF = 4096
NBLK = 16
NCH = 8
NT = 16
NG = 4
FFG = 4
P = 128
KB = 1024

DBG = False
PH = 6


def _build_tables():
    half = HD // 2
    fraction = (2 * np.arange(half, dtype=np.float32)) / np.float32(HD)
    timescale = np.float32(10000.0) ** fraction
    pos = np.arange(T, dtype=np.float32)
    ang = pos[:, None] / timescale[None, :]
    sin = np.sin(ang).astype(np.float32)
    cos = np.cos(ang).astype(np.float32)
    cosdup = np.empty((P, T), dtype=np.float32)
    sinsgn = np.empty((P, T), dtype=np.float32)
    for b in range(4):
        cosdup[32 * b:32 * b + 32] = cos.T
        sinsgn[32 * b:32 * b + 32] = (-sin.T) if b % 2 == 0 else sin.T
    ind = np.zeros((P, NCH, 16), dtype=bfnp)
    for c in range(NCH):
        ind[0:64, c, 2 * c] = 1
        ind[64:128, c, 2 * c + 1] = 1
    sel = np.zeros((NBLK, NBLK * P), dtype=bfnp)
    for i in range(NBLK):
        sel[i, P * i:P * (i + 1)] = 1
    return {
        "cosdup": cosdup.astype(bfnp),
        "sinsgn": sinsgn.astype(bfnp),
        "ident": np.eye(P, dtype=bfnp),
        "causal": np.triu(np.ones((P, P), np.float32)).astype(bfnp),
        "onesrow": np.ones((1, 512), dtype=bfnp),
        "ind": ind.reshape(P, NCH * 16),
        "sel": sel,
    }


class Stream:
    def __init__(self, name, sem, inc):
        self.name = name
        self.sem = sem
        self.inc = inc
        self.count = 0
        self.thunks = []
        self.waited = {}

    def wait(self, ev):
        if ev is None:
            return
        sname, sem, val = ev
        if val <= self.waited.get(sname, -1):
            return
        self.waited[sname] = val
        self.thunks.append(lambda eng, s=sem, v=val: eng.wait_ge(s, v))

    def emit(self, fn, milestone=True, inc=None):
        if milestone:
            step = self.inc if inc is None else inc
            self.count += step

            def th(eng, fn=fn, step=step):
                fn(eng).then_inc(self.sem, step)
            self.thunks.append(th)
            return (self.name, self.sem, self.count)
        self.thunks.append(lambda eng, fn=fn: fn(eng))
        return None

    def raw(self, fn):
        self.thunks.append(fn)

    def ev(self):
        return (self.name, self.sem, self.count)


def build_nc():
    nc = bass.Bass("TRN2", target_bir_lowering=False)

    def din(name, shape, dt=F32):
        return nc.dram_tensor(name, list(shape), dt, kind="ExternalInput").ap()

    x_d = din("x", [T, D])
    wq_d = din("wq", [D, D]); wk_d = din("wk", [D, D])
    wv_d = din("wv", [D, D]); wo_d = din("wo", [D, D])
    bqp_d = din("bq_pp", [P, NCH]); bkp_d = din("bk_pp", [P, NCH])
    bv_d = din("bv_row", [1, D]); bo_d = din("bo_row", [1, D])
    rwt_d = din("rwt", [D, NBLK]); rb_d = din("rb_row", [1, NBLK])
    w1t_d = din("w1t", [D, DFF]); b1p_d = din("b1_pp", [P, 32])
    w2t_d = din("w2t", [DFF, D]); b2_d = din("b2_row", [1, D])
    cos_d = din("cosdup", [P, T], BF16); sin_d = din("sinsgn", [P, T], BF16)
    id_d = din("ident", [P, P], BF16); ca_d = din("causal", [P, P], BF16)
    ones_d = din("onesrow", [1, 512], BF16)
    ind_d = din("ind", [P, NCH * 16], BF16)
    sel_d = din("sel", [NBLK, NBLK * P], BF16)

    negc_scr_t = nc.dram_tensor("negc_scr", [NH, 1], F32, kind="Internal")
    out_d = nc.dram_tensor("out", [T, D], F32, kind="ExternalOutput").ap()
    dbg_outs = {}
    if DBG:
        for nm, shape, dt in [
            ("d_lnx", [P, NCH * T], BF16), ("d_q", [P, NCH * T], BF16),
            ("d_k", [P, NCH * T], BF16), ("d_v", [P, NT * NH * 65], BF16),
            ("d_negc", [P, NH], F32), ("d_ctx", [P, NT * D], BF16),
            ("d_smA", [P, 64], F32), ("d_smB", [P, 64], F32),
            ("d_ncb", [16, 32], BF16),
            ("d_x2", [P, NT * D], F32), ("d_mask", [NBLK, T], BF16),
            ("d_xf", [P, NCH * T], BF16),
        ]:
            dbg_outs[nm] = nc.dram_tensor(nm, shape, dt,
                                          kind="ExternalOutput").ap()

    ARENA_BYTES = 206 * KB
    arena_cm = nc.sbuf_tensor("arena", [P, ARENA_BYTES], mybir.dt.uint8)
    sems_cm = [nc.semaphore(f"sem{i}") for i in range(6)]

    with arena_cm as arena, \
            sems_cm[0] as sp, sems_cm[1] as sa, sems_cm[2] as sv, \
            sems_cm[3] as sxm, sems_cm[4] as sg, sems_cm[5] as so, \
            nc.psum_tensor("pp", [P, 4096], F32) as pp, \
            nc.Block() as block:
        a_bf = arena.bitcast(BF16)
        a_f32 = arena.bitcast(F32)
        pp_bf = pp.bitcast(BF16)

        def c_bf(off_b, n, pr=P, p0=0):
            return a_bf[p0:p0 + pr, off_b // 2: off_b // 2 + n]

        def c_f32(off_b, n, pr=P, p0=0):
            return a_f32[p0:p0 + pr, off_b // 4: off_b // 4 + n]

        O = {}
        _o = [0]

        def region(name, nbytes):
            O[name] = _o[0]
            _o[0] += (nbytes + 63) & ~63

        region("ident", 256); region("causal", 256); region("onesrow", 1024)
        region("ind", 256); region("negc", 64); region("eps", 64)
        region("clamp", 64); region("bqp", 32); region("bkp", 32)
        region("b1p", 128)
        region("bv_hi", 2 * KB); region("bv_lo", 2 * KB)
        region("bo_hi", 2 * KB); region("bo_lo", 2 * KB)
        region("b2_hi", 2 * KB); region("b2_lo", 2 * KB)
        region("rb_hi", 64); region("rb_lo", 64)
        region("smallA", 256)
        region("smallB", 256)
        region("ncbf", 64)
        region("vxf", 36 * KB)
        region("qkx2", 64 * KB)
        region("lnctx", 32 * KB)
        region("wbuf", 16 * KB)
        region("xtile", 8 * KB)
        region("cossin", 8 * KB)
        region("shufsq", 8 * KB)
        region("dmisc", 8 * KB)
        region("obuf", 8 * KB)
        assert _o[0] <= ARENA_BYTES, _o[0]

        ident = c_bf(O["ident"], P)
        causal = c_bf(O["causal"], P)
        onesrow = c_bf(O["onesrow"], 512, pr=1)
        ind = c_bf(O["ind"], NCH * 16)
        negc = c_f32(O["negc"], NH)
        eps_pp = c_f32(O["eps"], 1)
        clamp_pp = c_f32(O["clamp"], 1)
        bqp = c_f32(O["bqp"], NCH)
        bkp = c_f32(O["bkp"], NCH)
        b1p = c_f32(O["b1p"], 32)
        brow = c_f32(O["obuf"], D, pr=1)
        brow2 = c_f32(O["obuf"] + 4 * KB, D, pr=1)
        bv_hi = c_bf(O["bv_hi"], D, pr=1); bv_lo = c_bf(O["bv_lo"], D, pr=1)
        bo_hi = c_bf(O["bo_hi"], D, pr=1); bo_lo = c_bf(O["bo_lo"], D, pr=1)
        b2_hi = c_bf(O["b2_hi"], D, pr=1); b2_lo = c_bf(O["b2_lo"], D, pr=1)
        rb_hi = c_bf(O["rb_hi"], NBLK, pr=1); rb_lo = c_bf(O["rb_lo"], NBLK, pr=1)
        smallA = c_f32(O["smallA"], 64)
        smallB = c_f32(O["smallB"], 64)
        nc_bf16 = c_bf(O["ncbf"], 32, pr=16)
        nb_sb = c_f32(O["dmisc"], 2048, pr=16)
        ctx_raw = c_f32(O["dmisc"], 2 * 264).rearrange(
            "p (h c) -> p h c", h=2)

        v_tm = c_bf(O["vxf"], NT * NH * 65).rearrange(
            "p (t h c) -> p t h c", t=NT, h=NH)
        xf_fm = c_bf(O["vxf"], NCH * T).rearrange("p (c t) -> p c t", c=NCH)
        mask_fm = c_bf(O["vxf"] + 32 * KB, T, pr=NBLK)
        q_fm = c_bf(O["qkx2"], NCH * T).rearrange("p (c t) -> p c t", c=NCH)
        k_fm = c_bf(O["qkx2"] + 32 * KB, NCH * T).rearrange(
            "p (c t) -> p c t", c=NCH)
        x2 = c_f32(O["qkx2"], NT * D).rearrange("p (t d) -> p t d", t=NT)
        lnx_fm = c_bf(O["lnctx"], NCH * T).rearrange("p (c t) -> p c t", c=NCH)
        ctx_tm = c_bf(O["lnctx"], NT * D).rearrange("p (t d) -> p t d", t=NT)
        h_fm = c_bf(O["lnctx"], 32 * 512).rearrange("p (c t) -> p c t", c=32)
        wbuf = c_bf(O["wbuf"], NCH * D).rearrange("p (c d) -> p c d", c=NCH)
        xt = c_f32(O["xtile"], 2 * D).rearrange("p (b d) -> p b d", b=2)
        cos_sb = c_bf(O["cossin"], T)
        sin_sb = c_bf(O["cossin"] + 4 * KB, T)
        p_sb = c_bf(O["cossin"], 2 * 2 * 1024).rearrange(
            "p (h b c) -> p h b c", h=2, b=2)
        ln2_tm = c_bf(O["cossin"], 4 * 1024).rearrange("p (b d) -> p b d", b=4)
        rwt_sb = c_bf(O["dmisc"] + 256, NCH * NBLK).rearrange(
            "p (c r) -> p c r", c=NCH)
        shuf = c_bf(O["shufsq"], T)
        lnsq = c_bf(O["shufsq"] + 4 * KB, 2 * 1024).rearrange(
            "p (b d) -> p b d", b=2)
        sq_sb = c_bf(O["shufsq"], T)
        ctx_fm2 = c_bf(O["shufsq"], 2 * NCH * 128).rearrange(
            "p (b c t) -> p b c t", b=2, c=NCH)
        w1buf = c_bf(O["shufsq"], 2 * 1024).rearrange(
            "p (b k) -> p b k", b=2)
        w2buf = c_bf(O["shufsq"] + 4 * KB, 2 * 1024).rearrange(
            "p (b k) -> p b k", b=2)
        w1g = c_bf(O["cossin"], 2 * 8 * 512).rearrange(
            "p (b c k) -> p b c k", b=2, c=NCH)
        w2g = c_bf(O["wbuf"], 2 * 4 * 1024).rearrange(
            "p (b i k) -> p b i k", b=2, i=4)
        mask_sb = c_bf(O["dmisc"], 2 * 512).rearrange(
            "p (b k) -> p b k", b=2)
        sel_sb = c_bf(O["dmisc"] + 4 * KB, NBLK * P, pr=NBLK)
        r_sb = c_f32(O["dmisc"], 16)
        r2_sb = c_f32(O["dmisc"] + 64, 16)
        m01_sb = c_f32(O["dmisc"] + 128, 16)
        m01_bf2 = c_bf(O["dmisc"] + 192, 2 * 16).rearrange(
            "p (b r) -> p b r", b=2)
        obuf = c_f32(O["obuf"], 2 * 512).rearrange("p (b d) -> p b d", b=2)

        def bank(b, n=512, p0=0, pr=P):
            return pp[p0:p0 + pr, 512 * b: 512 * b + n]

        def bank_off(b, off, n, p0=0, pr=P):
            return pp[p0:p0 + pr, 512 * b + off: 512 * b + off + n]

        S_pe = Stream("pe", sp, 1)
        S_act = Stream("act", sa, 1)
        S_dve = Stream("dve", sv, 1)
        S_sx = Stream("sx", sxm, 16)
        S_gw = Stream("gw", sg, 16)
        S_out = S_sx
        streams = [S_pe, S_act, S_dve, S_sx, S_gw]

        def barrier():
            evs = [s.ev() for s in streams]
            for s in streams:
                for e in evs:
                    s.wait(e)

        e_x_pre = [S_sx.emit(lambda e, i=i: e.dma_start(
            out=xt[:, i, :], in_=x_d[128 * i:128 * (i + 1), :]))
            for i in range(2)]
        for dst, src in [(ident, id_d), (causal, ca_d), (onesrow, ones_d),
                         (ind, ind_d), (bqp, bqp_d), (bkp, bkp_d),
                         (b1p, b1p_d), (cos_sb, cos_d), (sin_sb, sin_d)]:
            e_tables = S_sx.emit(
                lambda e, dst=dst, src=src: e.dma_start(out=dst, in_=src))
        for s in (S_pe, S_act, S_dve, S_gw):
            s.wait(e_tables)
        S_dve.emit(lambda e: e.memset(eps_pp, 1e-5), milestone=False)
        e_eps = S_dve.emit(lambda e: e.memset(clamp_pp, 1e-30))

        wq_r = wq_d.rearrange("(c p) d -> c p d", p=P)
        wk_r = wk_d.rearrange("(c p) d -> c p d", p=P)
        wv_r = wv_d.rearrange("(c p) d -> c p d", p=P)
        wo_r = wo_d.rearrange("(c p) d -> c p d", p=P)

        def load_w(w_r):
            evs = []
            for dc in range(NCH):
                evs.append(S_gw.emit(lambda e, dc=dc, w_r=w_r: e.dma_start(
                    out=wbuf[:, dc, :], in_=w_r[dc])))
            return evs

        e_wq = load_w(wq_r)


        e_lnx_tr = [None] * NT

        def ln_stats(src, dst_tm, e_src, cb):
            S_dve.wait(e_src)
            e_sum = S_dve.emit(lambda e: e.reduce_sum(
                smallA[:, cb:cb + 1], src, axis=AX.X))
            S_act.wait(e_sum)
            e_ssq = S_act.emit(lambda e: e.activation(
                dst_tm, src, AF.Square, accum_out=smallA[:, cb + 1:cb + 2]))
            return e_sum, e_ssq

        def ln_tail(src, dst_tm, e_sum, e_ssq, cb):
            S_dve.wait(e_ssq)
            S_dve.wait(e_sum)
            ev = S_dve.emit(lambda e: e.tensor_scalar(
                smallA[:, cb + 2:cb + 3], smallA[:, cb:cb + 1], 1.0 / D,
                None, ALU.mult))
            S_dve.wait(ev)
            ev = S_dve.emit(lambda e: e.tensor_mul(
                smallA[:, cb + 3:cb + 4], smallA[:, cb + 2:cb + 3],
                smallA[:, cb + 2:cb + 3]))
            S_dve.wait(ev)
            e_var = S_dve.emit(lambda e: e.scalar_tensor_tensor(
                smallA[:, cb + 4:cb + 5], smallA[:, cb + 1:cb + 2], 1.0 / D,
                smallA[:, cb + 3:cb + 4], ALU.mult, ALU.subtract))
            S_act.wait(e_var)
            S_act.wait(e_eps)
            e_lnv = S_act.emit(lambda e: e.activation(
                smallA[:, cb + 5:cb + 6], smallA[:, cb + 4:cb + 5], AF.Ln,
                bias=eps_pp))
            S_act.wait(e_lnv)
            e_rstd = S_act.emit(lambda e: e.activation(
                smallA[:, cb + 6:cb + 7], smallA[:, cb + 5:cb + 6], AF.Exp,
                scale=-0.5))
            S_dve.wait(e_rstd)
            e_nmr = S_dve.emit(lambda e: e.tensor_scalar(
                smallA[:, cb + 7:cb + 8], smallA[:, cb + 2:cb + 3],
                smallA[:, cb + 6:cb + 7], -1.0, ALU.mult, ALU.mult))
            S_dve.wait(e_nmr)
            e_ln = S_dve.emit(lambda e: e.tensor_scalar(
                dst_tm, src, smallA[:, cb + 6:cb + 7],
                smallA[:, cb + 7:cb + 8], ALU.mult, ALU.add))
            return e_ln

        def layernorm(src_getter, dst_tm, i, e_src):
            src = src_getter()
            cb = 8 * (i % 4)
            e_sum, e_ssq = ln_stats(src, dst_tm, e_src, cb)
            return ln_tail(src, dst_tm, e_sum, e_ssq, cb)

        LNT = int(_os.environ.get("LNT", str(NT)))
        LNTR = int(_os.environ.get("LNTR", "1"))
        if PH >= 1:
         ln_rec = {}
         e_lnevac_last = [None]
         eq_map = {}
         e_qlast_mm = [None]

         def ln1_tail_of(i):
            bi = i % 2
            e_sum, e_ssq, e_x = ln_rec[i]
            e_ln_tail = ln_tail(xt[:, bi, :], lnsq[:, bi, :], e_sum, e_ssq,
                                8 * (i % 2))
            e_tr = e_ln_tail
            if LNTR:
                S_pe.wait(e_ln_tail)
                S_pe.wait(e_lnevac_last[0])
                e_trp = None
                for dc in range(NCH):
                    tcol = 1024 * (4 + dc % 4) + 128 * (dc // 4)
                    e_trp = S_pe.emit(lambda e, i=i, bi=bi, dc=dc, tcol=tcol:
                                      e.transpose(
                                          pp_bf[:, tcol:tcol + 128],
                                          lnsq[:, bi,
                                               128 * dc:128 * (dc + 1)],
                                          ident))
                S_dve.wait(e_trp)
                for dc in range(NCH):
                    tcol = 1024 * (4 + dc % 4) + 128 * (dc // 4)
                    e_tr = S_dve.emit(lambda e, i=i, dc=dc, tcol=tcol:
                                      e.tensor_copy(
                                          lnx_fm[:, dc,
                                                 128 * i:128 * (i + 1)],
                                          pp_bf[:, tcol:tcol + 128]))
                e_lnevac_last[0] = e_tr
            e_lnx_tr[i] = e_tr
            return e_ln_tail

         def emit_qg(g):
            for c in range(NCH):
                bk = (g * NCH + c) % 4
                for dc in range(NCH):
                    S_pe.wait(e_wq[dc])
                S_pe.wait(e_lnx_tr[4 * g + 3])
                S_pe.wait(eq_map.get((g * NCH + c - 4, "ev")))

                def mms(e, c=c, g=g, bk=bk):
                    for dc in range(NCH):
                        last = e.matmul(
                            bank(bk), wbuf[:, dc, 128 * c:128 * (c + 1)],
                            lnx_fm[:, dc, 512 * g:512 * (g + 1)],
                            start=(dc == 0), stop=(dc == NCH - 1))
                    return last
                e_mm = S_pe.emit(mms)
                e_qlast_mm[0] = e_mm
                S_act.wait(e_mm)
                e_ev = S_act.emit(
                    lambda e, c=c, g=g, bk=bk: e.activation(
                        q_fm[:, c, 512 * g:512 * (g + 1)],
                        bank(bk), AF.Identity, bias=bqp[:, c:c + 1]))
                eq_map[(g * NCH + c, "ev")] = e_ev
                eq_map[(c, g)] = e_ev

         e_ln_tail = None
         for i in range(LNT):
            bi = i % 2
            if i < 2:
                e_x = e_x_pre[i]
            else:
                S_sx.wait(e_ln_tail)
                e_x = S_sx.emit(lambda e, i=i, bi=bi: e.dma_start(
                    out=xt[:, bi, :], in_=x_d[128 * i:128 * (i + 1), :]))
            if i >= 2:
                S_act.wait(e_lnx_tr[i - 2])
                S_dve.wait(e_lnx_tr[i - 2])
            ln_rec[i] = (*ln_stats(xt[:, bi, :], lnsq[:, bi, :], e_x,
                                   8 * (i % 2)), e_x)
            if i >= 1:
                e_ln_tail = ln1_tail_of(i - 1)
                if PH >= 2 and (i - 1) % 4 == 3:
                    emit_qg((i - 1) // 4)
         if LNT:
            e_ln_tail = ln1_tail_of(LNT - 1)
            if PH >= 2 and LNT == NT:
                emit_qg(3)

        browB = c_f32(O["dmisc"], D, pr=1)
        brow2B = c_f32(O["dmisc"] + 4 * KB, D, pr=1)

        def mk_hilo(src_d, hi, lo, n, sc, sc2, e_prev):
            S_sx.wait(e_prev)
            ev_l = S_sx.emit(lambda e, src_d=src_d, n=n, sc=sc: e.dma_start(
                out=sc[0:1, 0:n], in_=src_d))
            S_dve.wait(ev_l)
            e1 = S_dve.emit(lambda e, hi=hi, n=n, sc=sc: e.tensor_copy(
                hi, sc[0:1, 0:n]))
            S_dve.wait(e1)
            e2 = S_dve.emit(lambda e, hi=hi, n=n, sc2=sc2: e.tensor_copy(
                sc2[0:1, 0:n], hi))
            S_dve.wait(e2)
            return S_dve.emit(lambda e, lo=lo, n=n, sc=sc, sc2=sc2:
                              e.tensor_sub(lo, sc[0:1, 0:n], sc2[0:1, 0:n]))

        e_bv = mk_hilo(bv_d, bv_hi, bv_lo, D, brow, brow2, None)
        e_bo = mk_hilo(bo_d, bo_hi, bo_lo, D, browB, brow2B, None)
        e_b2 = mk_hilo(b2_d, b2_hi, b2_lo, D, brow, brow2, e_bv)
        e_rb = mk_hilo(rb_d, rb_hi, rb_lo, NBLK, browB, brow2B, e_bo)

        sq_sb2 = c_bf(O["shufsq"] + 4 * KB, T)

        wva = c_bf(O["xtile"], 4 * D).rearrange("p (c d) -> p c d", c=4)
        wvb = c_bf(O["obuf"], 4 * D).rearrange("p (c d) -> p c d", c=4)

        def wv_chunk(dc):
            return wva[:, dc, :] if dc < 4 else wvb[:, dc - 4, :]

        e_norm_last = [None]

        def emit_normchunk(src_fm, ev_src, c):
            S_dve.wait(ev_src)
            if e_norm_last[0] is not None:
                S_dve.wait(e_norm_last[0])
            if c == 0:
                S_pe.wait(e_lnx_tr[NT - 1])
            e_sq = S_dve.emit(lambda e, c=c, src_fm=src_fm: e.tensor_mul(
                sq_sb2, src_fm[:, c, :], src_fm[:, c, :]))
            S_pe.wait(e_sq)
            for tc in range(4):
                e_norm_last[0] = S_pe.emit(
                    lambda e, c=c, tc=tc: e.matmul(
                        bank(4 + tc, n=512, pr=16),
                        ind[:, 16 * c:16 * (c + 1)],
                        sq_sb2[:, 512 * tc:512 * (tc + 1)],
                        start=(c == 0), stop=(c == NCH - 1)))

        def norm_finish(dst):
            S_act.wait(e_norm_last[0])
            for tc in range(4):
                e_ev = S_act.emit(lambda e, tc=tc: e.activation(
                    nb_sb[0:16, 512 * tc:512 * (tc + 1)],
                    bank(4 + tc, pr=16), AF.Copy))
            S_dve.wait(e_ev)
            e_red = S_dve.emit(lambda e, dst=dst: e.reduce_max(
                dst[0:16, 12:13], nb_sb[0:16, :], axis=AX.X))
            return e_ev, e_red

        def emit_ropechunk(dst_fm, c, ev_src, e_prev_rope):
            S_sx.wait(ev_src)
            S_sx.wait(e_prev_rope)
            for bsrc, bdst in [(1, 0), (0, 1), (3, 2), (2, 3)]:
                e_sh = S_sx.emit(
                    lambda e, c=c, bsrc=bsrc, bdst=bdst, dst_fm=dst_fm:
                    e.dma_start(
                        out=shuf[32 * bdst:32 * (bdst + 1), :],
                        in_=dst_fm[32 * bsrc:32 * (bsrc + 1), c, :]))
            S_dve.wait(e_sh)
            S_dve.emit(lambda e: e.tensor_mul(shuf, shuf, sin_sb),
                       milestone=False)
            e_mul2 = S_dve.emit(lambda e, c=c, dst_fm=dst_fm: e.tensor_mul(
                dst_fm[:, c, :], dst_fm[:, c, :], cos_sb))
            S_dve.wait(e_mul2)
            return S_dve.emit(
                lambda e, c=c, dst_fm=dst_fm: e.tensor_add(
                    dst_fm[:, c, :], dst_fm[:, c, :], shuf))

        if PH >= 2:
          eq = [(None, eq_map[(c, g)]) for c in range(NCH) for g in range(NG)]
          S_gw.wait(e_qlast_mm[0])
          e_wk = load_w(wk_r)
          S_gw.wait(e_lnx_tr[NT - 1])
          S_gw.wait(e_rb)
          e_wv = [S_gw.emit(lambda e, dc=dc: e.dma_start(
              out=wv_chunk(dc), in_=wv_r[dc])) for dc in range(NCH)]
          e_rope_q = [None] * NCH
          e_prev_rope = [None]
          ek = []
          for g in range(NG):
              for c in range(NCH):
                  j = g * NCH + c
                  bk = j % 4
                  for dc in range(NCH):
                      S_pe.wait(e_wk[dc])
                  S_pe.wait(e_lnx_tr[4 * g + 3])
                  if len(ek) >= 4:
                      S_pe.wait(ek[-4][1])

                  def mms(e, c=c, g=g, bk=bk):
                      for dc in range(NCH):
                          last = e.matmul(
                              bank(bk),
                              wbuf[:, dc, 128 * c:128 * (c + 1)],
                              lnx_fm[:, dc, 512 * g:512 * (g + 1)],
                              start=(dc == 0), stop=(dc == NCH - 1))
                      return last
                  e_mm = S_pe.emit(mms)
                  S_act.wait(e_mm)
                  e_ev = S_act.emit(
                      lambda e, c=c, g=g, bk=bk: e.activation(
                          k_fm[:, c, 512 * g:512 * (g + 1)],
                          bank(bk), AF.Identity, bias=bkp[:, c:c + 1]))
                  ek.append((e_mm, e_ev))
                  if PH >= 3 and j % 2 == 1 and j < 16:
                      qc = j // 2
                      ev_src = eq[qc * NG + NG - 1][1]
                      emit_normchunk(q_fm, ev_src, qc)
                      e_rope_q[qc] = emit_ropechunk(
                          q_fm, qc, ev_src, e_prev_rope[0])
                      e_prev_rope[0] = e_rope_q[qc]
          if PH >= 3:
              e_qn_ev, e_qn = norm_finish(smallA)

          e_rope_k = [None] * NCH
          e_norm_last[0] = None
          ev_v = []
          for i in range(NT):
              for hf in range(2):
                  j = 2 * i + hf
                  bk = j % 4
                  for dc in range(NCH):
                      S_pe.wait(e_wv[dc])
                  S_pe.wait(e_bv)
                  S_pe.wait(e_lnx_tr[i])
                  if len(ev_v) >= 4:
                      S_pe.wait(ev_v[-4][1])

                  def mms(e, i=i, hf=hf, bk=bk):
                      for dc in range(NCH):
                          e.matmul(bank(bk),
                                   lnx_fm[:, dc, 128 * i:128 * (i + 1)],
                                   wv_chunk(dc)[:, 512 * hf:512 * (hf + 1)],
                                   start=(dc == 0), stop=False)
                      e.matmul(bank(bk), onesrow[0:1, 0:128],
                               bv_hi[0:1, 512 * hf:512 * (hf + 1)],
                               start=False, stop=False)
                      return e.matmul(bank(bk), onesrow[0:1, 0:128],
                                      bv_lo[0:1, 512 * hf:512 * (hf + 1)],
                                      start=False, stop=True)
                  e_mm = S_pe.emit(mms)
                  S_dve.wait(e_mm)
                  e_ev = S_dve.emit(lambda e, i=i, hf=hf, bk=bk: e.tensor_copy(
                      v_tm[:, i, 8 * hf:8 * (hf + 1), 0:64],
                      bank(bk).rearrange("p (h s) -> p h s", s=64)))
                  ev_v.append((e_mm, e_ev))
                  if PH >= 3 and j % 2 == 1 and j < 16:
                      kc = j // 2
                      ev_src = ek[3 * NCH + kc][1]
                      if kc == 0:
                          S_pe.wait(e_qn_ev)
                      emit_normchunk(k_fm, ev_src, kc)
                      e_rope_k[kc] = emit_ropechunk(
                          k_fm, kc, ev_src, e_prev_rope[0])
                      e_prev_rope[0] = e_rope_k[kc]
          e_vones = S_dve.emit(lambda e: e.memset(
              v_tm[:, :, :, 64:65], 1.0))
          if PH >= 5:
              S_gw.wait(ek[-1][0])
              e_wo = load_w(wo_r)

        if PH >= 3:
          S_act.wait(e_qn)
          e_kn_ev, e_kn = norm_finish(smallB)

          S_gw.wait(e_kn)
          S_gw.wait(e_qn)
          e_c2 = S_gw.emit(lambda e: e.tensor_mul(
              smallA[0:16, 13:14], smallA[0:16, 12:13], smallB[0:16, 12:13]))
          S_act.wait(e_c2)
          eps16 = c_f32(O["eps"], 1, pr=16)
          e_lnc = S_act.emit(lambda e: e.activation(
              smallA[0:16, 14:15], smallA[0:16, 13:14], AF.Ln,
              bias=eps16))
          S_act.wait(e_lnc)
          e_c8 = S_act.emit(lambda e: e.activation(
              smallA[0:16, 15:16], smallA[0:16, 14:15], AF.Exp, scale=0.5))
          S_gw.wait(e_c8)
          e_nmul = S_gw.emit(lambda e: e.tensor_scalar(
              smallB[0:16, 14:15], smallA[0:16, 15:16], -0.125, None,
              ALU.mult))
          S_sx.wait(e_nmul)
          e_nst = S_sx.emit(lambda e: e.dma_start(
              out=negc_scr_t.ap(), in_=smallB[0:16, 14:15]))
          S_sx.wait(e_nst)
          negc_bcast_src = bass.AP(negc_scr_t.ap().tensor, 0,
                                   [[0, P], [1, NH]])
          e_negc = S_sx.emit(lambda e: e.dma_start(
              out=negc, in_=negc_bcast_src))

        if DBG:
            barrier()
            taps = []
            if PH >= 1:
                taps.append(("d_lnx", c_bf(O["lnctx"], NCH * T)))
            if PH >= 2:
                taps.append(("d_v", c_bf(O["vxf"], NT * NH * 65)))
            if PH >= 3:
                taps += [("d_q", c_bf(O["qkx2"], NCH * T)),
                         ("d_k", c_bf(O["qkx2"] + 32 * KB, NCH * T)),
                         ("d_negc", negc), ("d_smA", smallA),
                         ("d_smB", smallB), ("d_ncb", nc_bf16)]
            if _os.environ.get("DBGTAP", "1") == "1":
                for nm, view in taps:
                    S_out.emit(lambda e, nm=nm, view=view: e.dma_start(
                        out=dbg_outs[nm], in_=view))

        barrier()
        if PH >= 4:
          PBBASE = [0, 2, 6]
          ABANKS = [[0, 1, 2], [3, 6, 7]]
          e_slot = {}
          e_psb = {}
          e_ctx_ev = {}
          for h01 in range(2):
              for sl in range(3):
                  e_slot[(h01, sl)] = None
              for sl in range(4):
                  e_psb[(h01, sl)] = None
          p_sbc = p_sb.rearrange("p h b c -> p (h b) c").rearrange(
              "p s (b c) -> p (s b) c", c=512)

          for c in range(NCH):
              for g in range(NG):
                  chunks = []
                  for j in range(4 * g + 4):
                      jrel = j - 4 * g
                      if jrel < 0:
                          chunks.append((j, 512, 512 * g))
                      else:
                          chunks.append((j, 512 - 128 * jrel,
                                         512 * g + 128 * jrel))
                  nj = len(chunks)

                  def emit_mm(j):
                      jj, w, qoff = chunks[j]
                      ev = {}
                      for h01 in range(2):
                          rb0 = 64 * h01
                          sb_ = ABANKS[h01][j % 3]
                          S_pe.wait(e_slot[(h01, j % 3)])
                          ev[h01] = S_pe.emit(
                              lambda e, c=c, rb0=rb0, sb_=sb_, jj=jj, w=w,
                              qoff=qoff: e.matmul(
                                  bank(sb_, n=w),
                                  k_fm[rb0:rb0 + 64, c,
                                       128 * jj:128 * (jj + 1)],
                                  q_fm[rb0:rb0 + 64, c, qoff:qoff + w],
                                  start=True, stop=True,
                                  tile_position=(rb0, 0)))
                      return ev

                  def emit_rest(j, g=g):
                      jj, w, qoff = chunks[j]
                      jrel0 = max(jj - 4 * g, 0)
                      for h01 in range(2):
                          h = 2 * c + h01
                          sb_ = ABANKS[h01][j % 3]
                          ps = 4 * h01 + j % 4
                          S_act.wait(mm_evs[j][h01])
                          S_act.wait(e_negc)
                          S_act.wait(e_psb[(h01, j % 4)])
                          e_exp = S_act.emit(
                              lambda e, h=h, sb_=sb_, ps=ps, w=w:
                              e.activation(
                                  p_sbc[:, ps, 0:w], bank(sb_, n=w),
                                  AF.Exp, bias=negc[:, h:h + 1], scale=0.125))
                          e_slot[(h01, j % 3)] = e_exp
                          e_mask = e_exp
                          if jj >= 4 * g:
                              S_dve.wait(e_exp)
                              e_mask = S_dve.emit(
                                  lambda e, ps=ps: e.tensor_mul(
                                      p_sbc[:, ps, 0:128],
                                      p_sbc[:, ps, 0:128], causal))
                          cbank = 4 + h01
                          last_pv = None
                          for t in range(jrel0, 4):
                              first = (jj == 0 and t == jrel0)
                              lastm = (jj == 4 * g + 3 and t == 3)
                              S_pe.wait(e_mask)
                              if first:
                                  S_pe.wait(e_ctx_ev.get(h01))
                              colp = (t - jrel0) * 128
                              last_pv = S_pe.emit(
                                  lambda e, h=h, ps=ps, t=t, jj=jj, colp=colp,
                                  cbank=cbank, first=first, lastm=lastm:
                                  e.matmul(
                                      bank_off(cbank, 66 * t, 65),
                                      p_sbc[:, ps, colp:colp + 128],
                                      v_tm[:, jj, h, :],
                                      start=first, stop=lastm))
                          e_psb[(h01, j % 4)] = last_pv
                          if jj == 4 * g + 3:
                              S_dve.wait(last_pv)
                              e_pk = S_dve.emit(
                                  lambda e, cbank=cbank, h01=h01:
                                  e.tensor_copy(ctx_raw[:, h01, :],
                                                bank(cbank, n=264)))
                              e_ctx_ev[h01] = e_pk
                              crv = ctx_raw[:, h01, :].rearrange(
                                  "p (t c) -> p t c", c=66)
                              S_act.wait(e_pk)
                              e_cl = S_act.emit(
                                  lambda e, crv=crv, h01=h01: e.activation(
                                      smallB[:, 16 + 8 * h01:20 + 8 * h01],
                                      crv[:, :, 64:65].rearrange(
                                          "p t c -> p (t c)"), AF.Copy))
                              S_dve.wait(e_cl)
                              S_dve.emit(
                                  lambda e, h01=h01: e.tensor_scalar(
                                      smallB[:, 16 + 8 * h01:20 + 8 * h01],
                                      smallB[:, 16 + 8 * h01:20 + 8 * h01],
                                      clamp_pp, None, ALU.max),
                                  milestone=False)
                              e_rc = S_dve.emit(
                                  lambda e, h01=h01: e.reciprocal(
                                      smallB[:, 20 + 8 * h01:24 + 8 * h01],
                                      smallB[:, 16 + 8 * h01:20 + 8 * h01]))
                              S_dve.wait(e_rc)
                              for t in range(4):
                                  S_dve.emit(
                                      lambda e, t=t, g=g, h=h, h01=h01,
                                      crv=crv: e.tensor_scalar(
                                          ctx_tm[:, 4 * g + t,
                                                 64 * h:64 * (h + 1)],
                                          crv[:, t, 0:64],
                                          smallB[:, 20 + 8 * h01 + t:
                                                 21 + 8 * h01 + t],
                                          None, ALU.mult))

                  mm_evs = {}
                  for j in range(nj + 1):
                      if j < nj:
                          mm_evs[j] = emit_mm(j)
                      if j >= 1:
                          emit_rest(j - 1)

        if DBG and PH >= 4:
            barrier()
            S_out.emit(lambda e: e.dma_start(
                out=dbg_outs["d_ctx"], in_=c_bf(O["lnctx"], NT * D)))

        if PH >= 5:
          barrier()
          rwt_r = rwt_d.rearrange("(c p) r -> c p r", p=P)
          e_rwt = []
          for dc in range(NCH):
              e_rwt.append(S_gw.emit(lambda e, dc=dc: e.dma_start(
                  out=rwt_sb[:, dc, :], in_=rwt_r[dc])))

          e_x2 = [None] * NT
          e_ln2 = [None] * NT
          e_xftr = [None] * NT
          e_ms = [None] * NT
          e_rmm = [None] * NT
          e_r2c = [None] * NT
          e_m01e = [None] * NT
          e_mtre = [None] * NT
          e_ctr_ev = {}
          e_xr_ev = {}
          e_omm_hist = {}

          def c_partA(i):
              bi = i % 2
              if i >= 2:
                  S_gw.wait(e_x2[i - 2])
              e_xr = S_gw.emit(lambda e, i=i, bi=bi: e.dma_start(
                  out=xt[:, bi, :], in_=x_d[128 * i:128 * (i + 1), :]))
              e_xr_ev[i] = e_xr
              cb = 4 + i % 2
              S_pe.wait(e_ctr_ev.get(i - 2))
              e_trp = None
              for dc in range(NCH):
                  tcol = 1024 * cb + 128 * dc
                  e_trp = S_pe.emit(lambda e, i=i, dc=dc, tcol=tcol:
                                    e.transpose(
                                        pp_bf[:, tcol:tcol + 128],
                                        ctx_tm[:, i, 128 * dc:128 * (dc + 1)],
                                        ident))
              S_dve.wait(e_trp)
              S_dve.wait(e_omm_hist.get(i - 2))
              ev_tr = None
              for dc in range(NCH):
                  tcol = 1024 * cb + 128 * dc
                  ev_tr = S_dve.emit(lambda e, i=i, dc=dc, tcol=tcol, bi=bi:
                                     e.tensor_copy(ctx_fm2[:, bi, dc, :],
                                                   pp_bf[:, tcol:tcol + 128]))
              e_ctr_ev[i] = ev_tr
              if i >= 1:
                  emit_oproj(i - 1)

          def emit_oproj(i):
              bi = i % 2
              for hf in range(2):
                  bk = hf
                  S_pe.wait(e_ctr_ev[i])
                  for dc in range(NCH):
                      S_pe.wait(e_wo[dc])
                  S_pe.wait(e_bo)
                  if i >= 1:
                      S_pe.wait(e_x2[i - 1])

                  def mms(e, hf=hf, bk=bk, bi=bi):
                      for dc in range(NCH):
                          e.matmul(bank(bk), ctx_fm2[:, bi, dc, :],
                                   wbuf[:, dc, 512 * hf:512 * (hf + 1)],
                                   start=(dc == 0), stop=False)
                      e.matmul(bank(bk), onesrow[0:1, 0:128],
                               bo_hi[0:1, 512 * hf:512 * (hf + 1)],
                               start=False, stop=False)
                      return e.matmul(bank(bk), onesrow[0:1, 0:128],
                                      bo_lo[0:1, 512 * hf:512 * (hf + 1)],
                                      start=False, stop=True)
                  e_omm = S_pe.emit(mms)
                  e_omm_hist[i] = e_omm
                  S_dve.wait(e_omm)
                  S_dve.wait(e_xr_ev[i])
                  e_x2i = S_dve.emit(lambda e, i=i, hf=hf, bk=bk,
                                     bi=bi: e.tensor_add(
                                         x2[:, i, 512 * hf:512 * (hf + 1)],
                                         bank(bk),
                                         xt[:, bi, 512 * hf:512 * (hf + 1)]))
              e_x2[i] = e_x2i
              if i >= 4:
                  S_act.wait(e_xftr[i - 4])
              e_ln2[i] = layernorm(lambda i=i: x2[:, i, :],
                                   ln2_tm[:, i % 4, :], i, e_x2[i])

          def c_partB1(i):
              xb = 6 + i % 2
              S_pe.wait(e_ln2[i])
              if i >= 2:
                  S_pe.wait(e_xftr[i - 2])
              e_trp = None
              for dc in range(NCH):
                  tcol = 1024 * xb + 128 * dc
                  e_trp = S_pe.emit(lambda e, i=i, dc=dc, tcol=tcol:
                                    e.transpose(
                                        pp_bf[:, tcol:tcol + 128],
                                        ln2_tm[:, i % 4,
                                               128 * dc:128 * (dc + 1)],
                                        ident))
              S_act.wait(e_trp)
              e_xtr = None
              for dc in range(NCH):
                  tcol = 1024 * xb + 128 * dc
                  e_xtr = S_act.emit(lambda e, i=i, dc=dc, tcol=tcol:
                                     e.activation(
                                         xf_fm[:, dc, 128 * i:128 * (i + 1)],
                                         pp_bf[:, tcol:tcol + 128], AF.Copy))
              e_xftr[i] = e_xtr

          def c_partB2a(i):
              S_pe.wait(e_xftr[i])
              for er in e_rwt:
                  S_pe.wait(er)
              S_pe.wait(e_rb)
              if i >= 1:
                  S_pe.wait(e_r2c[i - 1])

              def rmm(e, i=i):
                  for dc in range(NCH):
                      e.matmul(bank(2, n=16),
                               xf_fm[:, dc, 128 * i:128 * (i + 1)],
                               rwt_sb[:, dc, :], start=(dc == 0), stop=False)
                  e.matmul(bank(2, n=16), onesrow[0:1, 0:128], rb_hi[0:1, :],
                           start=False, stop=False)
                  return e.matmul(bank(2, n=16), onesrow[0:1, 0:128],
                                  rb_lo[0:1, :], start=False, stop=True)
              e_rmm[i] = S_pe.emit(rmm)
              S_dve.wait(e_rmm[i])
              if i >= 1:
                  S_dve.wait(e_m01e[i - 1])
              S_dve.emit(lambda e: e.tensor_copy(r_sb, bank(2, n=16)),
                         milestone=False)
              ev = S_dve.emit(lambda e: e.tensor_copy(r2_sb, bank(2, n=16)))
              e_r2c[i] = ev
              for _ in range(3):
                  S_dve.wait(ev)
                  ev = S_dve.emit(lambda e: e.reduce_max(
                      smallA[:, 32:33], r2_sb, axis=AX.X))
                  S_gw.wait(ev)
                  ev = S_gw.emit(lambda e: e.tensor_scalar(
                      m01_sb, r2_sb, smallA[:, 32:33], -1e30,
                      ALU.is_ge, ALU.mult))
                  S_gw.wait(ev)
                  ev = S_gw.emit(lambda e: e.tensor_add(
                      r2_sb, r2_sb, m01_sb))
              S_dve.wait(ev)
              ev = S_dve.emit(lambda e: e.reduce_max(
                  smallA[:, 32:33], r2_sb, axis=AX.X))
              S_gw.wait(ev)
              ev = S_gw.emit(lambda e: e.tensor_scalar(
                  m01_sb, r_sb, smallA[:, 32:33], None, ALU.is_ge))
              S_gw.wait(ev)
              if i >= 2:
                  S_gw.wait(e_mtre[i - 2])
              e_m01e[i] = S_gw.emit(lambda e, i=i: e.tensor_copy(
                  m01_bf2[:, i % 2, :], m01_sb))

          def c_partB2b(i):
              S_pe.wait(e_m01e[i])
              if i >= 1:
                  S_pe.wait(e_ms[i - 1])
              e_mtr = S_pe.emit(lambda e, i=i: e.matmul(
                  bank(3, n=128, pr=16), m01_bf2[:, i % 2, :], ident,
                  start=True, stop=True))
              e_mtre[i] = e_mtr
              S_dve.wait(e_mtr)
              e_ms[i] = S_dve.emit(lambda e, i=i: e.tensor_copy(
                  mask_fm[0:16, 128 * i:128 * (i + 1)],
                  bank(3, n=128, pr=16)))

          for it in range(NT + 7):
              if 0 <= it - 6 < NT:
                  c_partB2b(it - 6)
              if 0 <= it - 3 < NT:
                  c_partB1(it - 3)
              if it < NT:
                  c_partA(it)
              elif it == NT:
                  emit_oproj(NT - 1)
              if 0 <= it - 4 < NT:
                  c_partB2a(it - 4)
        if DBG and PH >= 5:
            barrier()
            S_out.emit(lambda e: e.dma_start(
                out=dbg_outs["d_x2"], in_=c_f32(O["qkx2"], NT * D)))
            S_out.emit(lambda e: e.dma_start(
                out=dbg_outs["d_mask"], in_=mask_fm))
            S_out.emit(lambda e: e.dma_start(
                out=dbg_outs["d_xf"], in_=c_bf(O["vxf"], NCH * T)))

        if PH >= 6:
          barrier()
          e_selload = S_sx.emit(lambda e: e.dma_start(out=sel_sb, in_=sel_d))
          S_pe.wait(e_selload)
          S_act.wait(e_selload)

          w1t_r = w1t_d.rearrange("(c p) o -> c p o", p=P)
          w2t_r = w2t_d.rearrange("(c p) d -> c p d", p=P)

          e_w1l = {}
          e_w2l = {}
          e_w1use = {}
          e_w2use = {}
          e_hev = {}
          e_mbe = [None, None]
          e_ye_bank = [None] * 8
          e_y_last_dve = None
          e_ymm_last = None
          ylast = []

          for G in range(FFG):
              t0 = 512 * G
              def load_w1(og):
                  wb = og % 2
                  S_gw.wait(e_w1use.get((G, og - 2)))
                  if G > 0 and og < 2:
                      S_gw.wait(e_w1use.get((G - 1, 6 + og)))
                  e_w1l[(G, og)] = S_gw.emit(
                      lambda e, og=og, wb=wb: e.dma_start(
                          out=w1g[:, wb, :, :],
                          in_=w1t_r[:, :, 512 * og:512 * (og + 1)].rearrange(
                              "c p o -> p c o")))

              load_w1(0)
              load_w1(1)
              for oc in range(32):
                  og = oc // 4
                  wb = og % 2
                  i_blk = oc // 2
                  mb = i_blk % 2
                  if oc % 2 == 0:
                      S_pe.wait(e_ms[4 * G + 3])
                      S_pe.wait(e_mbe[mb])
                      if G > 0 and oc == 0:
                          S_pe.wait(e_ye_bank[3])
                      e_mm = S_pe.emit(lambda e, i_blk=i_blk, t0=t0: e.matmul(
                          bank(3), sel_sb[0:16, 128 * i_blk:128 * (i_blk + 1)],
                          mask_fm[0:16, t0:t0 + 512], start=True, stop=True))
                      S_act.wait(e_mm)
                      S_act.wait(e_hev.get((G, oc - 3)))
                      if G > 0 and oc < 4:
                          S_act.wait(e_hev.get((G - 1, 28 + oc)))
                      e_mbe[mb] = S_act.emit(lambda e, mb=mb: e.activation(
                          mask_sb[:, mb, :], bank(3), AF.Copy))
                  bk = oc % 3
                  S_pe.wait(e_w1l[(G, og)])
                  S_pe.wait(e_xftr[4 * G + 3])
                  S_pe.wait(e_hev.get((G, oc - 3)))
                  if G > 0 and oc < 3:
                      S_pe.wait(e_ye_bank[oc % 3])
                      S_pe.wait(e_ymm_last)

                  def hmm(e, oc=oc, wb=wb, t0=t0, bk=bk):
                      for dc in range(NCH):
                          last = e.matmul(
                              bank(bk),
                              w1g[:, wb, dc, 128 * (oc % 4):128 * (oc % 4 + 1)],
                              xf_fm[:, dc, t0:t0 + 512],
                              start=(dc == 0), stop=(dc == NCH - 1))
                      return last
                  e_hmm = S_pe.emit(hmm)
                  e_w1use[(G, og)] = e_hmm
                  if oc % 4 == 3 and og + 2 < 8:
                      load_w1(og + 2)
                  S_act.wait(e_hmm)
                  if G > 0:
                      S_act.wait(e_ymm_last)
                  e_hr = S_act.emit(lambda e, oc=oc, bk=bk: e.activation(
                      h_fm[:, oc, :], bank(bk), AF.Relu,
                      bias=b1p[:, oc:oc + 1]))
                  S_dve.wait(e_hr)
                  S_dve.wait(e_mbe[mb])
                  e_hm = S_dve.emit(lambda e, oc=oc, mb=mb: e.tensor_mul(
                      h_fm[:, oc, :], h_fm[:, oc, :], mask_sb[:, mb, :]))
                  e_hev[(G, oc)] = e_hm

              def load_w2(ig):
                  wb = ig % 2
                  S_gw.wait(e_w2use.get((G, ig - 2)))
                  if G > 0 and ig < 2:
                      S_gw.wait(e_w2use.get((G - 1, 6 + ig)))
                  e_w2l[(G, ig)] = S_gw.emit(
                      lambda e, ig=ig, wb=wb: e.dma_start(
                          out=w2g[:, wb, :, :],
                          in_=w2t_r[4 * ig:4 * (ig + 1)].rearrange(
                              "i p d -> p i d")))

              load_w2(0)
              load_w2(1)
              for ib in range(32):
                  ig = ib // 4
                  wb = ig % 2
                  S_pe.wait(e_w2l[(G, ig)])
                  S_pe.wait(e_hev[(G, ib)])
                  if ib == 0:
                      S_pe.wait(e_hev[(G, 31)])
                      for bb in range(8):
                          S_pe.wait(e_ye_bank[bb])
                      S_pe.wait(e_mbe[0]); S_pe.wait(e_mbe[1])

                  def ymm(e, ib=ib, wb=wb):
                      last = None
                      for tt in range(4):
                          for hf in range(2):
                              last = e.matmul(
                                  bank(2 * tt + hf),
                                  h_fm[:, ib, 128 * tt:128 * (tt + 1)],
                                  w2g[:, wb, ib % 4, 512 * hf:512 * (hf + 1)],
                                  start=(ib == 0), stop=False)
                      return last
                  e_ymm = S_pe.emit(ymm)
                  e_w2use[(G, ig)] = e_ymm
                  if ib % 4 == 3 and ig + 2 < 8:
                      load_w2(ig + 2)
              e_ymm_last = e_ymm

              S_pe.wait(e_b2)

              def ybias(e):
                  last = None
                  for tt in range(4):
                      for hf in range(2):
                          bk2 = 2 * tt + hf
                          e.matmul(bank(bk2), onesrow[0:1, 0:128],
                                   b2_hi[0:1, 512 * hf:512 * (hf + 1)],
                                   start=False, stop=False)
                          last = e.matmul(bank(bk2), onesrow[0:1, 0:128],
                                          b2_lo[0:1, 512 * hf:512 * (hf + 1)],
                                          start=False, stop=True)
                  return last
              e_yb = S_pe.emit(ybias)

              n_ev = 0
              for tt in range(4):
                  i_abs = 4 * G + tt
                  for hf in range(2):
                      bk2 = 2 * tt + hf
                      ob = n_ev % 2
                      S_dve.wait(e_yb)
                      if len(ylast) >= 2:
                          S_dve.wait(ylast[-2])
                      e_ye = S_dve.emit(lambda e, i_abs=i_abs, hf=hf, bk2=bk2,
                                        ob=ob: e.tensor_add(
                                            obuf[:, ob, :], bank(bk2),
                                            x2[:, i_abs,
                                               512 * hf:512 * (hf + 1)]))
                      e_ye_bank[bk2] = e_ye
                      e_y_last_dve = e_ye
                      S_out.wait(e_ye)
                      e_st = S_out.emit(lambda e, i_abs=i_abs, hf=hf, ob=ob:
                                        e.dma_start(
                                            out=out_d[128 * i_abs:
                                                      128 * (i_abs + 1),
                                                      512 * hf:
                                                      512 * (hf + 1)],
                                            in_=obuf[:, ob, :]))
                      ylast.append(e_st)
                      n_ev += 1

        S_sx.raw(lambda eng, v=S_sx.count: eng.wait_ge(sxm, v))

        def run_stream(s):
            def body(eng):
                for th in s.thunks:
                    th(eng)
            return body

        block.tensor(run_stream(S_pe))
        block.scalar(run_stream(S_act))
        block.vector(run_stream(S_dve))

        block.sync(run_stream(S_sx))
        block.gpsimd(run_stream(S_gw))

    return nc


def kernel(**inputs):
    x = np.ascontiguousarray(np.asarray(inputs["L0_attn_x"], dtype=np.float32))

    def g(n):
        return np.ascontiguousarray(np.asarray(inputs[n], dtype=np.float32))

    wq, wk, wv, wo = (g(f"L0_attn_w{n}_w") for n in "qkvo")
    bq, bk, bv, bo = (g(f"L0_attn_w{n}_b") for n in "qkvo")
    rw, rb = g("L0_ffn_router_w"), g("L0_ffn_router_b")
    w1, b1 = g("L0_ffn_w1"), g("L0_ffn_b1")
    w2, b2 = g("L0_ffn_w2"), g("L0_ffn_b2")

    common = {
        "wq": wq, "wk": wk, "wv": wv, "wo": wo,
        "bq_pp": np.ascontiguousarray(bq.reshape(NCH, P).T),
        "bk_pp": np.ascontiguousarray(bk.reshape(NCH, P).T),
        "bv_row": bv.reshape(1, D).copy(), "bo_row": bo.reshape(1, D).copy(),
        "rwt": np.ascontiguousarray(rw.T), "rb_row": rb.reshape(1, NBLK).copy(),
        "w1t": np.ascontiguousarray(w1.T),
        "b1_pp": np.ascontiguousarray(b1.reshape(32, P).T),
        "w2t": np.ascontiguousarray(w2.T), "b2_row": b2.reshape(1, D).copy(),
    }
    common.update(_build_tables())

    nc = build_nc()
    in_maps = []
    for core in range(8):
        m = dict(common)
        m["x"] = np.ascontiguousarray(x[core])
        in_maps.append(m)
    res = run_bass_kernel_spmd(nc, in_maps, core_ids=list(range(8)))
    return np.stack([r["out"] for r in res.results], axis=0)
